# version 23
# baseline (speedup 1.0000x reference)
import numpy as np
import ml_dtypes

import concourse.bass as bass
import concourse.mybir as mybir
from concourse.bass import AP
from concourse.bass_utils import run_bass_kernel_spmd

F32 = mybir.dt.float32
BF16 = mybir.dt.bfloat16
AF = mybir.ActivationFunctionType
ALU = mybir.AluOpType
AX = mybir.AxisListType

T = 4096
E = 1024
LATD = 512
EPS = 1e-5
AR_BYTES = 209536


def ext_chunks(slot):
    return 2 * (16 - slot)


class Alloc:
    def __init__(self, nbytes):
        self.free_list = [[0, nbytes]]

    def alloc(self, n, align=64):
        for i, (a, b) in enumerate(self.free_list):
            s = (a + align - 1) // align * align
            if s + n <= b:
                self.free_list.pop(i)
                if a < s:
                    self.free_list.append([a, s])
                if s + n < b:
                    self.free_list.append([s + n, b])
                self.free_list.sort()
                return s
        raise RuntimeError(f"arena OOM: need {n}B, free={self.free_list}")

    def free(self, off, n):
        self.free_list.append([off, off + n])
        self.free_list.sort()
        merged = []
        for a, b in self.free_list:
            if merged and merged[-1][1] >= a:
                merged[-1][1] = max(merged[-1][1], b)
            else:
                merged.append([a, b])
        self.free_list = merged


class Buf:
    __slots__ = ("wr", "rds")

    def __init__(self):
        self.wr = None
        self.rds = []


class Reg:

    def __init__(self, K, name, nbytes, dtype):
        self.K = K
        self.nb = nbytes
        self.esz = 2 if dtype == BF16 else 4
        self.w = nbytes // self.esz
        self.off = K.A.alloc(nbytes)
        self.t = K.place(name, [128, self.w], dtype, self.off)

    def v(self, boff, cols, p0=0, p1=128):
        c0 = boff // self.esz
        return self.t[p0:p1, c0:c0 + cols]

    def ap(self, boff, dims):
        return AP(self.t, boff // self.esz, [[self.w, 128]] + dims)

    def free(self):
        self.K.A.free(self.off, self.nb)


class KB:
    def __init__(self):
        nc = bass.Bass("TRN2", target_bir_lowering=False)
        self.nc = nc
        self.cnt = {}
        self.waited = {}
        self.sem_ctx = []
        self.sems = {}
        self.A = Alloc(AR_BYTES)
        self.nreg = 0
        names = ["t", "v", "s", "g"]
        names += [f"dg{i}" for i in range(4)] + [f"dh{i}" for i in range(4)]
        for name in names:
            cm = nc.semaphore(name)
            self.sems[name] = cm.__enter__()
            self.sem_ctx.append(cm)
            self.cnt[name] = 0
        self.lane = {"dg": 0, "dh": 0}

    def init_arena(self):
        self.back = self.nc.alloc_sbuf_tensor("arena", [128, AR_BYTES], mybir.dt.uint8)
        self.base = self.nc.lookup_mloc(self.back).addr

    def place(self, name, shape, dtype, boff):
        self.nreg += 1
        h = self.nc.virtual_sbuf_tensor(f"{name}{self.nreg}", list(shape), dtype)
        self.nc._tensor(h.name, list(h.shape), dtype, type="SB")
        m = self.nc.lookup_mloc(h)
        m.allocated = True
        m.addr = self.base + boff
        return h

    def _wait(self, eng, en, tok):
        if tok is None:
            return
        s, v = tok
        if self.waited.get((en, s), 0) < v:
            eng.wait_ge(self.sems[s], v)
            self.waited[(en, s)] = v

    def _op(self, eng, en, sem, ins, outs, emit, n=1):
        for b in ins:
            self._wait(eng, en, b.wr)
        for b in outs:
            self._wait(eng, en, b.wr)
            for r in b.rds:
                self._wait(eng, en, r)
        instr = emit()
        self.cnt[sem] += n
        instr.then_inc(self.sems[sem], n)
        tok = (sem, self.cnt[sem])
        for b in ins:
            b.rds.append(tok)
        for b in outs:
            b.wr = tok
            b.rds = []
        return tok

    def top(self, ins, outs, emit):
        return self._op(self.nc.tensor, "t", "t", ins, outs, emit)

    def vop(self, ins, outs, emit):
        return self._op(self.nc.vector, "v", "v", ins, outs, emit)

    def sop(self, ins, outs, emit):
        return self._op(self.nc.scalar, "s", "s", ins, outs, emit)

    def _dma(self, eng, en, q, ins, outs, emit):
        sem = f"{q}{self.lane[q]}"
        self.lane[q] = (self.lane[q] + 1) % 4
        if self.cnt[sem] > 0:
            self._wait(eng, en, (sem, self.cnt[sem]))
        return self._op(eng, en, sem, ins, outs, emit, 16)

    def gdma(self, ins, outs, emit):
        return self._dma(self.nc.gpsimd, "g", "dg", ins, outs, emit)

    def hdma(self, ins, outs, emit):
        return self._dma(self.nc.sync, "dh", "dh", ins, outs, emit)

    def bar(self):
        signal = [("t", self.nc.tensor), ("v", self.nc.vector),
                  ("s", self.nc.scalar), ("g", self.nc.gpsimd)]
        toks = []
        for en, eng in signal:
            self._wait(eng, en, (en, self.cnt[en]))
            self.cnt[en] += 1
            eng.nop().then_inc(self.sems[en], 1)
            toks.append((en, self.cnt[en]))
        for en, eng in signal + [("dh", self.nc.sync)]:
            for tok in toks:
                self._wait(eng, en, tok)


class StopBuild(Exception):
    pass


def build_program(limit=1000):
    def ckpt(i):
        if i >= limit:
            raise StopBuild()
    K = KB()
    nc = K.nc
    TE, VE, SE, GE, SY = nc.tensor, nc.vector, nc.scalar, nc.gpsimd, nc.sync

    d = {}
    for name, shape in (
            ("i_xseq", (T, E)), ("i_xown", (2048, E)),
            ("i_cq", (128, 2048)), ("i_sq", (128, 2048)),
            ("i_ck", (128, T)), ("i_sk", (128, T)),
            ("i_wq", (E, E)), ("i_wqs", (E, E)), ("i_wd", (E, LATD)),
            ("i_wk", (LATD, E)), ("i_wks", (LATD, E)), ("i_wv", (LATD, E)),
            ("i_wo", (E, E)), ("i_gwt", (E, 8)),
            ("i_w1t", (8 * E, 1536)), ("i_w2t", (8 * 768, E)),
            ("i_ws1t", (E, 3072)), ("i_ws2t", (1536, E)),
            ("i_idn", (128, 128))):
        d[name] = nc.dram_tensor(name, list(shape), F32, kind="ExternalInput").ap()
    y = nc.dram_tensor("y", [2048, E], F32, kind="ExternalOutput").ap()

    idb_b = nc.alloc_sbuf_tensor("idb_b", [128, 128], BF16)
    idb_f = nc.alloc_sbuf_tensor("idb_f", [128, 128], F32)
    ones_f = nc.alloc_sbuf_tensor("ones_f", [128, 64], F32)
    eps_t = nc.alloc_sbuf_tensor("eps_t", [128, 1], F32)
    lnt = nc.alloc_sbuf_tensor("lnt", [128, 48], F32)
    sq_scr = nc.alloc_sbuf_tensor("sq_scr", [128, 1024], BF16)
    K.init_arena()
    assert nc.sbuf_bytes_remaining >= 0, nc.sbuf_bytes_remaining

    pA = nc.alloc_psum_tensor("pA", [128, 1024], F32)
    pB = nc.alloc_psum_tensor("pB", [128, 1024], F32)
    pC = nc.alloc_psum_tensor("pC", [128, 1024], F32)
    pD = nc.alloc_psum_tensor("pD", [128, 512], F32)
    pT = nc.alloc_psum_tensor("pT", [128, 1024], BF16)

    top, vop, sop, gdma, hdma, bar = K.top, K.vop, K.sop, K.gdma, K.hdma, K.bar
    bpA, bpA2, bpB, bpB2, bpC, bpC2, bpD = (Buf() for _ in range(7))
    _bpT = Buf()
    bpT = [_bpT, _bpT]
    b_idb, b_ones, b_eps = Buf(), Buf(), Buf()
    b_lnt = [Buf(), Buf()]
    b_lng = Buf()

    def wload(reg, dram, nchunk, ncols, buf, boff=0):
        in_ap = AP(dram.tensor, 0, [[ncols, 128], [ncols * 128, nchunk], [1, ncols]])
        out_ap = reg.ap(boff, [[ncols, nchunk], [1, ncols]])
        return gdma([], [buf], lambda oap=out_ap, iap=in_ap: GE.dma_start(oap, iap))

    def _body():
        hdma([], [b_idb], lambda: SY.dma_start(idb_f[:, :], d["i_idn"]))
        gdma([], [b_idb], lambda: GE.dma_start(idb_b[:, :], d["i_idn"]))
        vop([], [b_ones], lambda: VE.memset(ones_f[:, :], 1.0))
        vop([], [b_eps], lambda: VE.memset(eps_t[:, :], EPS))

        rx0b = Reg(K, "x0b", 8 * 8192, BF16)
        rxst = Reg(K, "xst", 6 * 2048, BF16)
        b_x0b = [Buf() for _ in range(8)]
        b_xst = [Buf() for _ in range(6)]

        def x0bv(c, c0, w):
            return rx0b.v(c * 8192 + c0 * 2, w)

        rwq = Reg(K, "wq", 16384, BF16)
        rwqs = Reg(K, "wqs", 16384, BF16)
        b_wq, b_wqs = Buf(), Buf()
        rcq = Reg(K, "cq", 8192, F32)
        rsq = Reg(K, "sq", 8192, F32)
        b_cq, b_sq = Buf(), Buf()
        hdma([], [b_cq], lambda: SY.dma_start(rcq.v(0, 2048), d["i_cq"]))
        hdma([], [b_sq], lambda: SY.dma_start(rsq.v(0, 2048), d["i_sq"]))
        ckpt(0.2)

        for q4 in range(8):
            if q4 == 4:
                wload(rwq, d["i_wq"], 8, 1024, b_wq)
            if q4 == 6:
                wload(rwqs, d["i_wqs"], 8, 1024, b_wqs)
            for j in range(4):
                st = q4 * 4 + j
                sl = st % 6
                gdma([], [b_xst[sl]], lambda st=st, sl=sl: GE.dma_start(
                    rxst.v(sl * 2048, 1024),
                    d["i_xseq"][st * 128:(st + 1) * 128, :]))
            ckpt(0.3 + q4 / 100.0)
            for c in range(8):
                ph = c % 2
                def tr_emit(q4=q4, c=c, ph=ph):
                    last = None
                    for j in range(4):
                        sl = (q4 * 4 + j) % 6
                        last = TE.transpose(
                            pT[:, ph * 512 + j * 128:ph * 512 + (j + 1) * 128],
                            rxst.v(sl * 2048 + c * 256, 128), idb_b[:, :])
                    return last
                top([b_xst[(q4 * 4 + j) % 6] for j in range(4)] + [b_idb], [bpT[ph]], tr_emit)
                ckpt(0.4 + (q4 * 8 + c) / 100.0)
                vop([bpT[ph]], [b_x0b[c]], lambda q4=q4, c=c, ph=ph: VE.tensor_copy(
                    x0bv(c, q4 * 512, 512), pT[:, ph * 512:(ph + 1) * 512]))
                ckpt(0.45 + (q4 * 8 + c) / 1000.0)
            ckpt(0.5 + q4 / 100.0)

        rq = Reg(K, "qcol", 8 * 4096, BF16)
        b_q = [Buf() for _ in range(8)]
        rrt = Reg(K, "rt", 4096, F32)
        b_rt = [Buf(), Buf()]

        def qv(dc, c0, w, p0=0, p1=128):
            return rq.v(dc * 4096 + c0 * 2, w, p0, p1)

        ckpt(1)
        bar()
        rxst.free()

        for dc in range(8):
            for tc in range(4):
                pj = tc % 2
                bA = bpA if pj == 0 else bpA2
                bB = bpB if pj == 0 else bpB2
                def q_emit(dc=dc, tc=tc, pj=pj, reg=rwq, ps=pA):
                    last = None
                    for dn in range(8):
                        rhs = rx0b.ap(dn * 8192 + (3840 - 1024 * tc) * 2,
                                      [[-256, 4], [1, 128]])
                        last = TE.matmul(
                            ps[:, pj * 512:(pj + 1) * 512],
                            reg.v(dn * 2048 + dc * 256, 128),
                            rhs, start=(dn == 0), stop=(dn == 7))
                    return last
                top([b_wq] + b_x0b, [bA], lambda f=q_emit: f())
                top([b_wqs] + b_x0b, [bB],
                    lambda dc=dc, tc=tc, pj=pj: q_emit(dc, tc, pj, rwqs, pB))
                vop([bA, b_cq], [b_rt[0]], lambda tc=tc, pj=pj: VE.tensor_mul(
                    rrt.v(0, 512), pA[:, pj * 512:(pj + 1) * 512],
                    rcq.v(tc * 2048, 512)))
                vop([bB, b_sq], [b_rt[1]], lambda tc=tc, pj=pj: VE.tensor_mul(
                    rrt.v(2048, 512), pB[:, pj * 512:(pj + 1) * 512],
                    rsq.v(tc * 2048, 512)))
                vop([b_rt[0], b_rt[1]], [b_q[dc]], lambda dc=dc, tc=tc: VE.tensor_add(
                    qv(dc, tc * 512, 512), rrt.v(0, 512), rrt.v(2048, 512)))
        ckpt(2)
        bar()
        rwq.free(); rwqs.free(); rcq.free(); rsq.free()

        rwd = Reg(K, "wd", 8192, BF16)
        rlat = Reg(K, "lat", 4 * 8192, BF16)
        b_wd = Buf()
        b_lat = [Buf() for _ in range(4)]
        wload(rwd, d["i_wd"], 8, 512, b_wd)

        for ldc in range(4):
            for tc in range(8):
                pj = tc % 2
                bC = bpC if pj == 0 else bpC2
                def lat_emit(ldc=ldc, tc=tc, pj=pj):
                    last = None
                    for dn in range(8):
                        last = TE.matmul(
                            pC[:, pj * 512:(pj + 1) * 512],
                            rwd.v(dn * 1024 + ldc * 256, 128),
                            x0bv(dn, tc * 512, 512), start=(dn == 0), stop=(dn == 7))
                    return last
                top([b_wd] + b_x0b, [bC], lat_emit)
                vop([bC], [b_lat[ldc]], lambda ldc=ldc, tc=tc, pj=pj: VE.tensor_copy(
                    rlat.v(ldc * 8192 + tc * 1024, 512), pC[:, pj * 512:(pj + 1) * 512]))

        rwk = Reg(K, "wk", 8192, BF16)
        rwks = Reg(K, "wks", 8192, BF16)
        rwv = Reg(K, "wv", 8192, BF16)
        b_wk, b_wks, b_wv = Buf(), Buf(), Buf()
        wload(rwk, d["i_wk"], 4, 1024, b_wk)
        wload(rwks, d["i_wks"], 4, 1024, b_wks)
        wload(rwv, d["i_wv"], 4, 1024, b_wv)
        rck = Reg(K, "ck", 16384, F32)
        rsk = Reg(K, "sk", 16384, F32)
        b_ck, b_sk = Buf(), Buf()
        hdma([], [b_ck], lambda: SY.dma_start(rck.v(0, 4096), d["i_ck"]))
        hdma([], [b_sk], lambda: SY.dma_start(rsk.v(0, 4096), d["i_sk"]))
        ckpt(3)
        bar()

        rwd.free()
        kgo = 0
        aoo = 2 * 8192
        K.A.free(rx0b.off + 6 * 8192, 16384)
        rvg = Reg(K, "vg", 16640, BF16)
        raos = Reg(K, "aos", 2048, BF16)
        rpe = Reg(K, "pe", 2 * 2048, BF16)
        rpo = Reg(K, "po", 2 * 2048, BF16)
        rden = Reg(K, "den", 2 * 2048, F32)
        b_kg = [Buf(), Buf()]
        b_ao, b_aos, b_vg = Buf(), Buf(), Buf()
        b_pe = [Buf(), Buf()]
        b_po = [Buf(), Buf()]
        b_den = [Buf(), Buf()]

        def kgv(j, c0, w, p0=0, p1=128):
            return rx0b.v(kgo + j * 8192 + c0 * 2, w, p0, p1)

        def aov(boff, cols, p0=0, p1=128):
            return rx0b.v(aoo + boff, cols, p0, p1)

        for g in range(4):
            for j, dc in enumerate((2 * g, 2 * g + 1)):
                for tc in range(8):
                    pj = tc % 2
                    bA = bpA if pj == 0 else bpA2
                    bB = bpB if pj == 0 else bpB2
                    def k_emit(dc=dc, tc=tc, pj=pj, reg=rwk, ps=pA):
                        last = None
                        for ldc in range(4):
                            last = TE.matmul(
                                ps[:, pj * 512:(pj + 1) * 512],
                                reg.v(ldc * 2048 + dc * 256, 128),
                                rlat.v(ldc * 8192 + tc * 1024, 512),
                                start=(ldc == 0), stop=(ldc == 3))
                        return last
                    top([b_wk] + b_lat, [bA], lambda f=k_emit: f())
                    top([b_wks] + b_lat, [bB],
                        lambda dc=dc, tc=tc, pj=pj: k_emit(dc, tc, pj, rwks, pB))
                    vop([bA, b_ck], [b_rt[0]], lambda tc=tc, pj=pj: VE.tensor_mul(
                        rrt.v(0, 512), pA[:, pj * 512:(pj + 1) * 512],
                        rck.v(tc * 2048, 512)))
                    vop([bB, b_sk], [b_rt[1]], lambda tc=tc, pj=pj: VE.tensor_mul(
                        rrt.v(2048, 512), pB[:, pj * 512:(pj + 1) * 512],
                        rsk.v(tc * 2048, 512)))
                    vop([b_rt[0], b_rt[1]], [b_kg[j]], lambda j=j, tc=tc: VE.tensor_add(
                        kgv(j, tc * 512, 512), rrt.v(0, 512), rrt.v(2048, 512)))
            for tt in range(32):
                pj = tt % 2
                ps = pD if pj == 0 else pC
                bps = bpD if pj == 0 else bpC
                def v_emit(tt=tt, g=g, ps=ps):
                    last = None
                    for ldc in range(4):
                        last = TE.matmul(
                            ps[:, 0:256],
                            rlat.v(ldc * 8192 + tt * 256, 128),
                            rwv.v(ldc * 2048 + g * 512, 256),
                            start=(ldc == 0), stop=(ldc == 3))
                    return last
                top([b_wv] + b_lat, [bps], v_emit)
                vop([], [b_vg], lambda tt=tt: VE.memset(
                    rvg.ap(tt * 520 + 128, [[65, 4], [1, 1]]), 1.0))
                vop([bps, b_vg], [b_vg], lambda tt=tt, ps=ps: VE.tensor_copy(
                    rvg.ap(tt * 520, [[65, 4], [1, 64]]),
                    AP(ps, 0, [[ps.shape[1], 128], [64, 4], [1, 64]])))
            for j, dc in enumerate((2 * g, 2 * g + 1)):
                def epilogue(qg, dc=dc):
                    for p in (0, 1):
                        obuf = bpC if p == 0 else bpC2
                        ooff = 0 if p == 0 else 512
                        dof = p * 2048
                        sop([obuf], [b_den[p]], lambda ooff=ooff, dof=dof: SE.copy(
                            rden.v(dof, 512, 0, 65), pC[0:65, ooff:ooff + 512]))
                        sop([b_den[p]], [b_den[p]], lambda dof=dof: SE.activation(
                            rden.v(dof, 512, 64, 65), rden.v(dof, 512, 64, 65), AF.Ln))
                        sop([b_den[p]], [b_den[p]], lambda dof=dof: SE.activation(
                            rden.v(dof, 512, 64, 65), rden.v(dof, 512, 64, 65),
                            AF.Exp, scale=-1.0))
                        top([b_den[p], b_ones], [bpD], lambda dof=dof: TE.matmul(
                            pD[0:64, 0:512], ones_f[64:65, 0:64],
                            rden.v(dof, 512, 64, 65), start=True, stop=True))
                        if p == 0:
                            vop([b_den[p], bpD], [b_ao], lambda dc=dc, qg=qg, dof=dof:
                                VE.tensor_mul(
                                    aov(dc * 4096 + qg * 1024, 512, 0, 64),
                                    rden.v(dof, 512, 0, 64), pD[0:64, 0:512]))
                        else:
                            vop([b_den[p], bpD], [b_aos], lambda qg=qg, dof=dof:
                                VE.tensor_mul(
                                    raos.v((qg % 2) * 1024, 512, 0, 64),
                                    rden.v(dof, 512, 0, 64), pD[0:64, 0:512]))
                    if qg % 2 == 1:
                        gdma([b_aos], [b_ao], lambda dc=dc, qg=qg: GE.dma_start(
                            aov(dc * 4096 + (qg - 1) * 1024, 1024, 64, 128),
                            raos.v(0, 1024, 0, 64)))

                def emit_av(prev):
                    qg, c0, n0, n1, sl, last = prev
                    nch = ext_chunks(4 * qg)
                    for p in (0, 1):
                        rp = rpe if p == 0 else rpo
                        bP = b_pe[sl] if p == 0 else b_po[sl]
                        obuf = bpC if p == 0 else bpC2
                        ooff = 0 if p == 0 else 512
                        def av1(g=g, j=j, p=p, c0=c0, n0=n0, n1=n1, sl=sl,
                                rp=rp, ooff=ooff, nch=nch):
                            hh = j * 2 + p
                            lastins = None
                            for ci, (off, w) in enumerate(((0, n0), (n0, n1))):
                                if w <= 0:
                                    continue
                                c = c0 + ci
                                lastins = TE.matmul(
                                    pC[0:65, ooff:ooff + w],
                                    rvg.v(c * 520 + hh * 130, 65),
                                    rp.v(sl * 2048 + off * 2, w),
                                    start=(c == 0), stop=(c == nch - 1))
                            return lastins
                        top([bP, b_vg], [obuf], av1)
                    if last:
                        epilogue(qg)

                prev = None
                sl = 0
                for qg in range(4):
                    nch = ext_chunks(4 * qg)
                    for c0 in range(0, nch, 2):
                        na = [128 * sum(1 for i in range(4 * qg, 4 * qg + 4)
                                        if ext_chunks(i) > c) for c in (c0, c0 + 1)]
                        n0, n1 = na
                        segs = [(0, n0, c0)]
                        if n0 >= 512 or n0 + n1 <= 512:
                            segs.append((n0, n1, c0 + 1))
                        else:
                            segs.append((n0, 512 - n0, c0 + 1))
                            segs.append((512, n0 + n1 - 512, c0 + 1))
                        for p in (0, 1):
                            S = pA if p == 0 else pB
                            bS = [bpA, bpA2] if p == 0 else [bpB, bpB2]
                            def sc_emit(j=j, dc=dc, qg=qg, p=p, S=S,
                                        segs=tuple(segs), n0=n0, c0=c0):
                                lastins = None
                                for off, w, c in segs:
                                    if w <= 0:
                                        continue
                                    qs = off if c == c0 else off - n0
                                    lastins = TE.matmul(
                                        S[:, off:off + w],
                                        kgv(j, c * 128, 128, p * 64, (p + 1) * 64),
                                        qv(dc, qg * 512 + qs, w, p * 64, (p + 1) * 64),
                                        start=True, stop=True)
                                return lastins
                            top([b_kg[j], b_q[dc]], bS, sc_emit)
                        for p in (0, 1):
                            S = pA if p == 0 else pB
                            bS = [bpA, bpA2] if p == 0 else [bpB, bpB2]
                            rp = rpe if p == 0 else rpo
                            bP = b_pe[sl] if p == 0 else b_po[sl]
                            sop(bS, [bP], lambda S=S, n=n0 + n1, rp=rp, sl=sl:
                                SE.activation(rp.v(sl * 2048, n), S[:, 0:n], AF.Exp))
                        if prev is not None:
                            emit_av(prev)
                        prev = (qg, c0, n0, n1, sl, c0 + 2 >= nch)
                        sl = 1 - sl
                emit_av(prev)
            ckpt(5 if g == 0 else 6)
            bar()

        rvg.free(); raos.free(); rpe.free(); rpo.free(); rden.free()
        rlat.free(); rq.free(); rrt.free()
        rwk.free(); rwks.free(); rwv.free(); rck.free(); rsk.free()

        rwo = Reg(K, "wo", 16384, BF16)
        b_wo = Buf()
        wload(rwo, d["i_wo"], 8, 1024, b_wo)
        rx1 = Reg(K, "x1", 16 * 4096, F32)
        rxo = Reg(K, "xor", 2 * 4096, F32)
        b_x1 = [Buf() for _ in range(16)]
        b_xor = [Buf(), Buf()]

        def x1v(tt, c0, w):
            return rx1.v(tt * 4096 + c0 * 4, w)

        K.A.free(rx0b.off, 2 * 8192)
        rx1b = Reg(K, "x1b", 8 * 2048, BF16)
        rxlo = Reg(K, "xlo", 8 * 1024, BF16)
        rxbtm = Reg(K, "xbtm", 2 * 2048, BF16)
        rgwh = Reg(K, "gwh", 128, BF16)
        rgwl = Reg(K, "gwl", 128, BF16)
        rgwtf = Reg(K, "gwtf", 256, F32)
        rwcol = Reg(K, "wcol", 512, F32)
        rgst = Reg(K, "gst", 2048, F32)
        rgsc = Reg(K, "gsc", 64, F32)
        b_x1b, b_xlo = Buf(), Buf()
        b_xbtm = [Buf(), Buf()]
        b_gwh, b_gwl, b_gwtf, b_gst, b_gsc = Buf(), Buf(), Buf(), Buf(), Buf()
        b_w = [Buf() for _ in range(16)]

        hdma([], [b_gwtf], lambda: SY.dma_start(
            rgwtf.ap(0, [[8, 8], [1, 8]]),
            AP(d["i_gwt"].tensor, 0, [[8, 128], [8 * 128, 8], [1, 8]])))
        vop([b_gwtf], [b_gwh], lambda: VE.tensor_copy(
            rgwh.v(0, 64), rgwtf.v(0, 64)))
        vop([b_gwtf, b_gwh], [b_gwl], lambda: VE.tensor_sub(
            rgwl.v(0, 64), rgwtf.v(0, 64), rgwh.v(0, 64)))

        def build_tile(hf, tl):
            tt = hf * 8 + tl
            sl = tl % 2
            vop([b_x1[tt]], [b_xbtm[sl]], lambda tt=tt, sl=sl: VE.tensor_copy(
                rxbtm.v(sl * 2048, 1024), x1v(tt, 0, 1024)))

            def tx_emit(sl=sl):
                last = None
                for c in range(8):
                    last = TE.transpose(
                        pT[:, c * 128:(c + 1) * 128],
                        rxbtm.v(sl * 2048 + c * 256, 128), idb_b[:, :])
                return last
            top([b_xbtm[sl], b_idb], bpT, tx_emit)
            vop(bpT, [b_x1b], lambda tl=tl: VE.tensor_copy(
                rx1b.ap(tl * 256, [[1024, 8], [1, 128]]),
                AP(pT, 0, [[1024, 128], [128, 8], [1, 128]])))
            vop([b_x1[tt], b_xbtm[sl]], [b_xbtm[sl]], lambda tt=tt, sl=sl:
                VE.tensor_sub(
                    rxbtm.v(sl * 2048, 1024), x1v(tt, 0, 1024),
                    rxbtm.v(sl * 2048, 1024)))
            top([b_xbtm[sl], b_idb], bpT, tx_emit)
            vop(bpT, [b_xlo], lambda tl=tl: VE.tensor_copy(
                rxlo.ap((tl % 4) * 256, [[512, 8], [1, 128]]),
                AP(pT, 0, [[1024, 128], [128, 8], [1, 128]])))
            if tl % 4 == 3:
                tc = tl // 4

                def g_emit(tc=tc):
                    last = None
                    k = 0
                    for reg, isxlo in ((rgwh, False), (rgwl, False), (rgwh, True)):
                        for dn in range(8):
                            xap = (rxlo.v(dn * 1024, 512) if isxlo
                                   else rx1b.v(dn * 2048 + tc * 1024, 512))
                            last = TE.matmul(
                                pD[0:8, 0:512], reg.v(dn * 16, 8), xap,
                                start=(k == 0), stop=(k == 23))
                            k += 1
                    return last
                top([b_gwh, b_gwl, b_x1b, b_xlo], [bpD], g_emit)
                vop([bpD], [b_gst], lambda: VE.tensor_copy(
                    rgst.v(0, 512, 0, 8), pD[0:8, 0:512]))

                def gt_emit():
                    last = None
                    for jj in range(4):
                        last = TE.transpose(
                            pD[:, (jj + 4) * 8:(jj + 5) * 8],
                            rgst.v(jj * 512, 128, 0, 8), idb_f[0:8, 0:8])
                    return last
                top([b_gst, b_idb], [bpD], gt_emit)
                for jj in range(4):
                    ti = hf * 8 + tc * 4 + jj
                    gate_softmax_top2(K, pD, jj + 4, rwcol, rgsc, ti, lnt,
                                      bpD, b_w[ti], b_gsc, b_lng)

        def xo_dma(tt):
            sl = tt % 2
            hdma([], [b_xor[sl]], lambda tt=tt, sl=sl: SY.dma_start(
                rxo.v(sl * 4096, 1024), d["i_xown"][tt * 128:(tt + 1) * 128, :]))

        xo_dma(0)
        xo_dma(1)
        for tt in range(16):
            sl = tt % 2
            for dh in range(2):
                bA = bpA if dh == 0 else bpA2
                def op_emit(tt=tt, dh=dh):
                    last = None
                    for dc in range(8):
                        last = TE.matmul(
                            pA[:, dh * 512:(dh + 1) * 512],
                            aov(dc * 4096 + tt * 256, 128),
                            rwo.v(dc * 2048 + dh * 1024, 512),
                            start=(dc == 0), stop=(dc == 7))
                    return last
                top([b_ao, b_wo], [bA], op_emit)
                vop([bA, b_xor[sl]], [b_x1[tt]], lambda tt=tt, sl=sl, dh=dh:
                    VE.tensor_add(
                        x1v(tt, dh * 512, 512), pA[:, dh * 512:(dh + 1) * 512],
                        rxo.v(sl * 4096 + dh * 2048, 512)))
            if tt + 2 < 16:
                xo_dma(tt + 2)
            ln_tile(K, x1v, tt, sq_scr, lnt, eps_t, b_lnt, b_eps, b_x1[tt])
            if 0 <= tt - 2 < 8:
                build_tile(0, tt - 2)
        ckpt(7)
        bar()
        K.A.free(rx0b.off + 2 * 8192, 4 * 8192)
        rxo.free()

        ru = Reg(K, "u", 6 * 2048, BF16)
        rw1 = [Reg(K, "w1a", 24576, BF16), Reg(K, "w1b", 24576, BF16)]
        rw2 = Reg(K, "w2", 12288, BF16)
        rsg = Reg(K, "sg", 2 * 2048, F32)
        rtmp = Reg(K, "tmp", 2 * 2048, F32)
        rout = Reg(K, "out", 2 * 4096, F32)
        b_u = Buf()
        b_w1 = [Buf(), Buf()]
        b_w2 = Buf()
        b_sg = [Buf(), Buf()]
        b_tmp = [Buf(), Buf()]
        b_out = [Buf(), Buf()]

        def w1_dma(slot, ei):
            if ei < 8:
                iap = AP(d["i_w1t"].tensor, ei * 1024 * 1536,
                         [[1536, 128], [1536 * 128, 8], [1, 1536]])
            else:
                iap = AP(d["i_ws1t"].tensor, (ei - 8) * 1536,
                         [[3072, 128], [3072 * 128, 8], [1, 1536]])
            oap = rw1[slot].ap(0, [[1536, 8], [1, 1536]])
            gdma([], [b_w1[slot]], lambda oap=oap, iap=iap: GE.dma_start(oap, iap))

        def w2_dma(ei):
            if ei < 8:
                iap = AP(d["i_w2t"].tensor, ei * 768 * 1024,
                         [[1024, 128], [1024 * 128, 6], [1, 1024]])
            else:
                iap = AP(d["i_ws2t"].tensor, (ei - 8) * 768 * 1024,
                         [[1024, 128], [1024 * 128, 6], [1, 1024]])
            oap = rw2.ap(0, [[1024, 6], [1, 1024]])
            gdma([], [b_w2], lambda oap=oap, iap=iap: GE.dma_start(oap, iap))

        w1_dma(0, 0)
        w2_dma(0)

        for hf in range(2):
            ckpt(8 if hf == 0 else 10)
            for ei in range(10):
                slot = ei % 2
                nxt = ei + 1 if ei < 9 else (0 if hf == 0 else None)
                if nxt is not None:
                    w1_dma(1 - slot, nxt)
                for m in range(6):
                    for tc in range(2):
                        pj = (m * 2 + tc) % 2
                        ps = pA if pj == 0 else pB
                        bY = bpA if pj == 0 else bpB
                        bG = bpA2 if pj == 0 else bpB2
                        def w1_emit(m=m, tc=tc, slot=slot, ps=ps, blk=0):
                            last = None
                            for dn in range(8):
                                last = TE.matmul(
                                    ps[:, blk * 512:(blk + 1) * 512],
                                    rw1[slot].v(dn * 3072 + (2 * m + blk) * 256, 128),
                                    rx1b.v(dn * 2048 + tc * 1024, 512),
                                    start=(dn == 0), stop=(dn == 7))
                            return last
                        top([b_w1[slot], b_x1b], [bY], lambda f=w1_emit: f())
                        top([b_w1[slot], b_x1b], [bG],
                            lambda m=m, tc=tc, slot=slot, ps=ps: w1_emit(m, tc, slot, ps, 1))
                        ssl = (m * 2 + tc) % 2
                        sop([bG], [b_sg[ssl]], lambda ps=ps, ssl=ssl: SE.activation(
                            rsg.v(ssl * 2048, 512), ps[:, 512:1024], AF.Silu))
                        vop([bY, b_sg[ssl]], [b_u], lambda m=m, tc=tc, ps=ps, ssl=ssl:
                            VE.tensor_mul(
                                ru.v(m * 2048 + tc * 1024, 512),
                                ps[:, 0:512], rsg.v(ssl * 2048, 512)))
                for tl in range(8):
                    tt = hf * 8 + tl
                    for dh in range(2):
                        pj = (tl * 2 + dh) % 2
                        bZ = bpC if pj == 0 else bpC2
                        zo = 0 if pj == 0 else 512
                        def w2_emit(tl=tl, dh=dh, zo=zo):
                            last = None
                            for m in range(6):
                                last = TE.matmul(
                                    pC[:, zo:zo + 512],
                                    ru.v(m * 2048 + tl * 256, 128),
                                    rw2.v(m * 2048 + dh * 1024, 512),
                                    start=(m == 0), stop=(m == 5))
                            return last
                        top([b_w2, b_u], [bZ], w2_emit)
                        if ei < 8:
                            ssl = (tl * 2 + dh) % 2
                            sop([bZ, b_w[tt]], [b_tmp[ssl]],
                                lambda zo=zo, ssl=ssl, tt=tt, ei=ei: SE.activation(
                                    rtmp.v(ssl * 2048, 512),
                                    pC[:, zo:zo + 512], AF.Identity,
                                    scale=rwcol.v(tt * 32 + ei * 4, 1)))
                            vop([b_tmp[ssl], b_x1[tt]], [b_x1[tt]],
                                lambda tt=tt, dh=dh, ssl=ssl: VE.tensor_add(
                                    x1v(tt, dh * 512, 512), x1v(tt, dh * 512, 512),
                                    rtmp.v(ssl * 2048, 512)))
                        else:
                            vop([bZ, b_x1[tt]], [b_x1[tt]],
                                lambda tt=tt, dh=dh, zo=zo: VE.tensor_add(
                                    x1v(tt, dh * 512, 512), pC[:, zo:zo + 512],
                                    x1v(tt, dh * 512, 512)))
                    if ei == 9:
                        sl = tl % 2
                        ln_tile(K, x1v, tt, sq_scr, lnt, eps_t, b_lnt, b_eps,
                                b_x1[tt], out=(rout.v(sl * 4096, 1024), b_out[sl]))
                        hdma([b_out[sl]], [], lambda tt=tt, sl=sl: SY.dma_start(
                            y[tt * 128:(tt + 1) * 128, :], rout.v(sl * 4096, 1024)))
                        if hf == 0 and tl >= 1:
                            build_tile(1, tl - 1)
                if nxt is not None:
                    w2_dma(nxt)
                if ei == 9 and hf == 0:
                    build_tile(1, 7)
                ckpt(9 if (hf == 0 and ei == 0) else 99)

        ckpt(98)

    with nc.Block():
        try:
            _body()
        except StopBuild:
            pass
        for i in range(4):
            SY.wait_ge(K.sems[f"dh{i}"], K.cnt[f"dh{i}"])
            GE.wait_ge(K.sems[f"dg{i}"], K.cnt[f"dg{i}"])

    for cm in K.sem_ctx:
        cm.__exit__(None, None, None)
    return nc


def ln_tile(K, x1v, tt, sq_scr, lnt, eps_t, b_lnt_l, b_eps, b_x1, out=None):
    SE, VE = K.nc.scalar, K.nc.vector
    vop, sop = K.vop, K.sop
    b_lnt = b_lnt_l[tt % 2]
    o = (tt % 2) * 16

    def c(i):
        return lnt[:, o + i:o + i + 1]
    vop([b_x1], [b_lnt], lambda: VE.reduce_sum(c(0), x1v(tt, 0, 1024), axis=AX.X))
    sop([b_x1], [b_lnt], lambda: SE.activation(
        sq_scr[:, :], x1v(tt, 0, 1024), AF.Square, accum_out=c(1)))
    vop([b_lnt], [b_lnt], lambda: VE.tensor_scalar_mul(c(2), c(0), 1.0 / 1024.0))
    vop([b_lnt], [b_lnt], lambda: VE.tensor_scalar_mul(c(3), c(1), 1.0 / 1024.0))
    vop([b_lnt], [b_lnt], lambda: VE.tensor_mul(c(4), c(2), c(2)))
    vop([b_lnt], [b_lnt], lambda: VE.tensor_sub(c(5), c(3), c(4)))
    sop([b_lnt, b_eps], [b_lnt], lambda: SE.activation(
        c(6), c(5), AF.Ln, bias=eps_t[:, 0:1]))
    sop([b_lnt], [b_lnt], lambda: SE.activation(c(7), c(6), AF.Exp, scale=-0.5))
    vop([b_lnt], [b_lnt], lambda: K.nc.vector.scalar_tensor_tensor(
        c(8), c(2), -1.0, c(7), ALU.mult, ALU.mult))
    if out is None:
        sop([b_lnt], [b_x1], lambda: SE.activation(
            x1v(tt, 0, 1024), x1v(tt, 0, 1024), AF.Identity,
            bias=c(8), scale=c(7)))
    else:
        oap, b_o = out
        sop([b_lnt, b_x1], [b_o], lambda: SE.activation(
            oap, x1v(tt, 0, 1024), AF.Identity,
            bias=c(8), scale=c(7)))


def gate_softmax_top2(K, pC, jj, rwcol, rgsc, ti, lnt, bpC, b_w, b_gsc, b_lnt):
    VE, SE = K.nc.vector, K.nc.scalar
    vop, sop = K.vop, K.sop
    wv = lambda: rwcol.v(ti * 32, 8)
    gs = lambda: rgsc.v(0, 8)

    def c(i):
        return lnt[:, 32 + i:32 + i + 1]
    vop([bpC], [b_w], lambda: VE.tensor_copy(wv(), pC[:, jj * 8:(jj + 1) * 8]))
    vop([b_w], [b_lnt], lambda: VE.reduce_max(c(0), wv(), axis=AX.X))
    vop([b_w, b_lnt], [b_w], lambda: VE.tensor_scalar(
        wv(), wv(), c(0), None, ALU.subtract))
    sop([b_w], [b_w], lambda: SE.activation(wv(), wv(), AF.Exp))
    vop([b_w], [b_lnt], lambda: VE.reduce_sum(c(1), wv(), axis=AX.X))
    vop([b_lnt], [b_lnt], lambda: VE.reciprocal(c(2), c(1)))
    vop([b_w, b_lnt], [b_w], lambda: VE.tensor_scalar(
        wv(), wv(), c(2), None, ALU.mult))
    vop([b_w], [b_lnt], lambda: VE.reduce_max(c(3), wv(), axis=AX.X))
    vop([b_w, b_lnt], [b_gsc], lambda: VE.tensor_scalar(
        gs(), wv(), c(3), None, ALU.is_equal))
    vop([b_gsc, b_w], [b_gsc], lambda: VE.scalar_tensor_tensor(
        gs(), gs(), -1e30, wv(), ALU.mult, ALU.add))
    vop([b_gsc], [b_lnt], lambda: VE.reduce_max(c(4), gs(), axis=AX.X))
    vop([b_w, b_lnt], [b_gsc], lambda: VE.tensor_scalar(
        gs(), wv(), c(4), None, ALU.is_ge))
    vop([b_gsc, b_w], [b_w], lambda: VE.tensor_mul(wv(), wv(), gs()))


_CACHED_NC = None


def _get_nc():
    global _CACHED_NC
    if _CACHED_NC is None:
        _CACHED_NC = build_program()
    return _CACHED_NC


def _perm_deint():
    p = np.zeros(E, np.int64)
    for h in range(16):
        for jx in range(64):
            p[h * 64 + jx] = h * 64 + (2 * jx if jx < 32 else 2 * (jx - 32) + 1)
    return p


def _perm_swap():
    p = np.zeros(E, np.int64)
    for h in range(16):
        for jx in range(64):
            p[h * 64 + jx] = h * 64 + (jx + 32 if jx < 32 else jx - 32)
    return p


def kernel(L0_attn_x, L0_attn_wq, L0_attn_w_down, L0_attn_wk_up, L0_attn_wv_up,
           L0_attn_wo, L0_ffn_gate_w, L0_ffn_W1, L0_ffn_W2, L0_ffn_Ws1,
           L0_ffn_Ws2):
    x = np.asarray(L0_attn_x, np.float32)
    wq = np.asarray(L0_attn_wq, np.float32)
    wd = np.asarray(L0_attn_w_down, np.float32)
    wk = np.asarray(L0_attn_wk_up, np.float32)
    wv = np.asarray(L0_attn_wv_up, np.float32)
    wo = np.asarray(L0_attn_wo, np.float32)
    gw = np.asarray(L0_ffn_gate_w, np.float32)
    W1 = np.asarray(L0_ffn_W1, np.float32)
    W2 = np.asarray(L0_ffn_W2, np.float32)
    Ws1 = np.asarray(L0_ffn_Ws1, np.float32)
    Ws2 = np.asarray(L0_ffn_Ws2, np.float32)

    pd_ = _perm_deint()
    psw = _perm_swap()
    C = np.ascontiguousarray
    wq_p = C(wq[:, pd_])
    wq_s = C(wq_p[:, psw])
    wk_p = C(wk[:, pd_])
    wk_s = C(wk_p[:, psw])

    ro1 = np.concatenate([np.r_[j * 128:(j + 1) * 128, 768 + j * 128:768 + (j + 1) * 128]
                          for j in range(6)])
    W1t = C(np.concatenate([W1[e][ro1].T for e in range(8)], axis=0))
    W2t = C(np.concatenate([W2[e].T for e in range(8)], axis=0))
    ro2 = np.concatenate([np.r_[j * 128:(j + 1) * 128, 1536 + j * 128:1536 + (j + 1) * 128]
                          for j in range(12)])
    Ws1t = C(Ws1[ro2].T)
    Ws2t = C(Ws2.T)
    gwt = C(gw.T)
    idn = np.eye(128, dtype=np.float32)

    inv = 1.0 / (10000.0 ** (np.arange(0, 64, 2, dtype=np.float32) / 64))
    ang = np.outer(np.arange(T, dtype=np.float32), inv)
    cosf = np.cos(ang).astype(np.float32)
    sinf = np.sin(ang).astype(np.float32)
    rsel = np.arange(128) % 32
    sgn = np.where((np.arange(128) % 64) < 32, -1.0, 1.0).astype(np.float32)[:, None]

    shared = dict(i_wq=wq_p, i_wqs=wq_s, i_wd=wd, i_wk=wk_p, i_wks=wk_s,
                  i_wv=wv, i_wo=wo, i_gwt=gwt, i_w1t=W1t, i_w2t=W2t,
                  i_ws1t=Ws1t, i_ws2t=Ws2t, i_idn=idn)

    in_maps = []
    meta = []
    for core in range(8):
        seq, half = core // 2, core % 2
        idx_r = np.zeros(T, np.int64)
        for p in range(16):
            idx_r[p * 256:p * 256 + 128] = p * 256 + half * 128 + np.arange(128)
            idx_r[p * 256 + 128:(p + 1) * 256] = p * 256 + (1 - half) * 128 + np.arange(128)
        idx_own = np.concatenate([(15 - i) * 256 + np.arange(128) for i in range(16)])
        own_pos = idx_r[idx_own]
        xr = C(x[seq][idx_r])
        m = dict(shared)
        m.update(i_xseq=xr, i_xown=C(xr[idx_own]),
                 i_ck=C(cosf[idx_r].T[rsel]),
                 i_sk=C(sinf[idx_r].T[rsel] * sgn),
                 i_cq=C(0.125 * cosf[own_pos].T[rsel]),
                 i_sq=C(0.125 * sinf[own_pos].T[rsel] * sgn))
        in_maps.append(m)
        meta.append((seq, own_pos))

    nc = _get_nc()
    res = run_bass_kernel_spmd(nc, in_maps, core_ids=list(range(8)))

    ys = [np.asarray(res.results[core]["y"]) for core in range(8)]
    odt = np.float64 if ys[0].dtype == np.float64 else np.float32
    out = np.zeros((4, T, E), odt)
    for core in range(8):
        seq, pos = meta[core]
        out[seq, pos, :] = ys[core]
    return out


# version 24
# speedup vs baseline: 1.0472x; 1.0472x over previous
import numpy as np
import ml_dtypes

import concourse.bass as bass
import concourse.mybir as mybir
from concourse.bass import AP
from concourse.bass_utils import run_bass_kernel_spmd

F32 = mybir.dt.float32
BF16 = mybir.dt.bfloat16
AF = mybir.ActivationFunctionType
ALU = mybir.AluOpType
AX = mybir.AxisListType

T = 4096
E = 1024
LATD = 512
EPS = 1e-5
AR_BYTES = 209536


def ext_chunks(slot):
    return 2 * (16 - slot)


class Alloc:
    def __init__(self, nbytes):
        self.free_list = [[0, nbytes]]

    def alloc(self, n, align=64):
        for i, (a, b) in enumerate(self.free_list):
            s = (a + align - 1) // align * align
            if s + n <= b:
                self.free_list.pop(i)
                if a < s:
                    self.free_list.append([a, s])
                if s + n < b:
                    self.free_list.append([s + n, b])
                self.free_list.sort()
                return s
        raise RuntimeError(f"arena OOM: need {n}B, free={self.free_list}")

    def free(self, off, n):
        self.free_list.append([off, off + n])
        self.free_list.sort()
        merged = []
        for a, b in self.free_list:
            if merged and merged[-1][1] >= a:
                merged[-1][1] = max(merged[-1][1], b)
            else:
                merged.append([a, b])
        self.free_list = merged


class Buf:
    __slots__ = ("wr", "rds")

    def __init__(self):
        self.wr = None
        self.rds = []


class Reg:

    def __init__(self, K, name, nbytes, dtype):
        self.K = K
        self.nb = nbytes
        self.esz = 2 if dtype == BF16 else 4
        self.w = nbytes // self.esz
        self.off = K.A.alloc(nbytes)
        self.t = K.place(name, [128, self.w], dtype, self.off)

    def v(self, boff, cols, p0=0, p1=128):
        c0 = boff // self.esz
        return self.t[p0:p1, c0:c0 + cols]

    def ap(self, boff, dims):
        return AP(self.t, boff // self.esz, [[self.w, 128]] + dims)

    def free(self):
        self.K.A.free(self.off, self.nb)


class KB:
    def __init__(self):
        nc = bass.Bass("TRN2", target_bir_lowering=False)
        self.nc = nc
        self.cnt = {}
        self.waited = {}
        self.sem_ctx = []
        self.sems = {}
        self.A = Alloc(AR_BYTES)
        self.nreg = 0
        names = ["t", "v", "s", "g"]
        names += [f"dg{i}" for i in range(4)] + [f"dh{i}" for i in range(4)]
        for name in names:
            cm = nc.semaphore(name)
            self.sems[name] = cm.__enter__()
            self.sem_ctx.append(cm)
            self.cnt[name] = 0
        self.lane = {"dg": 0, "dh": 0}

    def init_arena(self):
        self.back = self.nc.alloc_sbuf_tensor("arena", [128, AR_BYTES], mybir.dt.uint8)
        self.base = self.nc.lookup_mloc(self.back).addr

    def place(self, name, shape, dtype, boff):
        self.nreg += 1
        h = self.nc.virtual_sbuf_tensor(f"{name}{self.nreg}", list(shape), dtype)
        self.nc._tensor(h.name, list(h.shape), dtype, type="SB")
        m = self.nc.lookup_mloc(h)
        m.allocated = True
        m.addr = self.base + boff
        return h

    def _wait(self, eng, en, tok):
        if tok is None:
            return
        s, v = tok
        if self.waited.get((en, s), 0) < v:
            eng.wait_ge(self.sems[s], v)
            self.waited[(en, s)] = v

    def _op(self, eng, en, sem, ins, outs, emit, n=1):
        for b in ins:
            self._wait(eng, en, b.wr)
        for b in outs:
            self._wait(eng, en, b.wr)
            for r in b.rds:
                self._wait(eng, en, r)
        instr = emit()
        self.cnt[sem] += n
        instr.then_inc(self.sems[sem], n)
        tok = (sem, self.cnt[sem])
        for b in ins:
            b.rds.append(tok)
        for b in outs:
            b.wr = tok
            b.rds = []
        return tok

    def top(self, ins, outs, emit):
        return self._op(self.nc.tensor, "t", "t", ins, outs, emit)

    def vop(self, ins, outs, emit):
        return self._op(self.nc.vector, "v", "v", ins, outs, emit)

    def sop(self, ins, outs, emit):
        return self._op(self.nc.scalar, "s", "s", ins, outs, emit)

    def _dma(self, eng, en, q, ins, outs, emit):
        sem = f"{q}{self.lane[q]}"
        self.lane[q] = (self.lane[q] + 1) % 4
        if self.cnt[sem] > 0:
            self._wait(eng, en, (sem, self.cnt[sem]))
        return self._op(eng, en, sem, ins, outs, emit, 16)

    def gdma(self, ins, outs, emit):
        return self._dma(self.nc.gpsimd, "g", "dg", ins, outs, emit)

    def hdma(self, ins, outs, emit):
        return self._dma(self.nc.sync, "dh", "dh", ins, outs, emit)

    def bar(self):
        signal = [("t", self.nc.tensor), ("v", self.nc.vector),
                  ("s", self.nc.scalar), ("g", self.nc.gpsimd)]
        toks = []
        for en, eng in signal:
            self._wait(eng, en, (en, self.cnt[en]))
            self.cnt[en] += 1
            eng.nop().then_inc(self.sems[en], 1)
            toks.append((en, self.cnt[en]))
        for en, eng in signal + [("dh", self.nc.sync)]:
            for tok in toks:
                self._wait(eng, en, tok)


class StopBuild(Exception):
    pass


def build_program(limit=1000):
    def ckpt(i):
        if i >= limit:
            raise StopBuild()
    K = KB()
    nc = K.nc
    TE, VE, SE, GE, SY = nc.tensor, nc.vector, nc.scalar, nc.gpsimd, nc.sync

    d = {}
    for name, shape in (
            ("i_xseq", (T, E)), ("i_xown", (2048, E)),
            ("i_cq", (128, 2048)), ("i_sq", (128, 2048)),
            ("i_ck", (128, T)), ("i_sk", (128, T)),
            ("i_wq", (E, E)), ("i_wqs", (E, E)), ("i_wd", (E, LATD)),
            ("i_wk", (LATD, E)), ("i_wks", (LATD, E)), ("i_wv", (LATD, E)),
            ("i_wo", (E, E)), ("i_gwt", (E, 8)),
            ("i_w1t", (8 * E, 1536)), ("i_w2t", (8 * 768, E)),
            ("i_ws1t", (E, 3072)), ("i_ws2t", (1536, E)),
            ("i_idn", (128, 128))):
        d[name] = nc.dram_tensor(name, list(shape), F32, kind="ExternalInput").ap()
    y = nc.dram_tensor("y", [2048, E], F32, kind="ExternalOutput").ap()

    idb_b = nc.alloc_sbuf_tensor("idb_b", [128, 128], BF16)
    idb_f = nc.alloc_sbuf_tensor("idb_f", [128, 128], F32)
    ones_f = nc.alloc_sbuf_tensor("ones_f", [128, 64], F32)
    eps_t = nc.alloc_sbuf_tensor("eps_t", [128, 1], F32)
    lnt = nc.alloc_sbuf_tensor("lnt", [128, 48], F32)
    sq_scr = nc.alloc_sbuf_tensor("sq_scr", [128, 1024], BF16)
    K.init_arena()
    assert nc.sbuf_bytes_remaining >= 0, nc.sbuf_bytes_remaining

    pA = nc.alloc_psum_tensor("pA", [128, 1024], F32)
    pB = nc.alloc_psum_tensor("pB", [128, 1024], F32)
    pC = nc.alloc_psum_tensor("pC", [128, 1024], F32)
    pD = nc.alloc_psum_tensor("pD", [128, 512], F32)
    pT = nc.alloc_psum_tensor("pT", [128, 1024], BF16)

    top, vop, sop, gdma, hdma, bar = K.top, K.vop, K.sop, K.gdma, K.hdma, K.bar
    bpA, bpA2, bpB, bpB2, bpC, bpC2, bpD = (Buf() for _ in range(7))
    _bpT = Buf()
    bpT = [_bpT, _bpT]
    b_idb, b_ones, b_eps = Buf(), Buf(), Buf()
    b_lnt = [Buf(), Buf()]
    b_lng = Buf()

    def wload(reg, dram, nchunk, ncols, buf, boff=0):
        in_ap = AP(dram.tensor, 0, [[ncols, 128], [ncols * 128, nchunk], [1, ncols]])
        out_ap = reg.ap(boff, [[ncols, nchunk], [1, ncols]])
        return gdma([], [buf], lambda oap=out_ap, iap=in_ap: GE.dma_start(oap, iap))

    def _body():
        hdma([], [b_idb], lambda: SY.dma_start(idb_f[:, :], d["i_idn"]))
        gdma([], [b_idb], lambda: GE.dma_start(idb_b[:, :], d["i_idn"]))
        vop([], [b_ones], lambda: VE.memset(ones_f[:, :], 1.0))
        vop([], [b_eps], lambda: VE.memset(eps_t[:, :], EPS))

        rx0b = Reg(K, "x0b", 8 * 8192, BF16)
        rxst = Reg(K, "xst", 6 * 2048, BF16)
        b_x0b = [Buf() for _ in range(8)]
        b_xst = [Buf() for _ in range(6)]

        def x0bv(c, c0, w):
            return rx0b.v(c * 8192 + c0 * 2, w)

        rwq = Reg(K, "wq", 16384, BF16)
        rwqs = Reg(K, "wqs", 16384, BF16)
        b_wq, b_wqs = Buf(), Buf()
        rcq = Reg(K, "cq", 8192, F32)
        rsq = Reg(K, "sq", 8192, F32)
        b_cq, b_sq = Buf(), Buf()
        hdma([], [b_cq], lambda: SY.dma_start(rcq.v(0, 2048), d["i_cq"]))
        hdma([], [b_sq], lambda: SY.dma_start(rsq.v(0, 2048), d["i_sq"]))
        ckpt(0.2)

        rq = Reg(K, "qcol", 8 * 4096, BF16)
        b_q = [Buf() for _ in range(8)]
        rrt = Reg(K, "rt", 4096, F32)
        b_rt = [Buf(), Buf()]

        def qv(dc, c0, w, p0=0, p1=128):
            return rq.v(dc * 4096 + c0 * 2, w, p0, p1)

        def emit_pair(dc, tc):
            pj = dc % 2
            bA = bpA if pj == 0 else bpA2
            bB = bpB if pj == 0 else bpB2

            def q_emit(dc=dc, tc=tc, pj=pj, reg=rwq, ps=pA):
                last = None
                for dn in range(8):
                    rhs = rx0b.ap(dn * 8192 + (3840 - 1024 * tc) * 2,
                                  [[-256, 4], [1, 128]])
                    last = TE.matmul(
                        ps[:, pj * 512:(pj + 1) * 512],
                        reg.v(dn * 2048 + dc * 256, 128),
                        rhs, start=(dn == 0), stop=(dn == 7))
                return last
            top([b_wq] + b_x0b, [bA], lambda f=q_emit: f())
            top([b_wqs] + b_x0b, [bB],
                lambda dc=dc, tc=tc, pj=pj: q_emit(dc, tc, pj, rwqs, pB))
            vop([bA, b_cq], [b_rt[0]], lambda tc=tc, pj=pj: VE.tensor_mul(
                rrt.v(0, 512), pA[:, pj * 512:(pj + 1) * 512],
                rcq.v(tc * 2048, 512)))
            vop([bB, b_sq], [b_rt[1]], lambda tc=tc, pj=pj: VE.tensor_mul(
                rrt.v(2048, 512), pB[:, pj * 512:(pj + 1) * 512],
                rsq.v(tc * 2048, 512)))
            vop([b_rt[0], b_rt[1]], [b_q[dc]], lambda dc=dc, tc=tc: VE.tensor_add(
                qv(dc, tc * 512, 512), rrt.v(0, 512), rrt.v(2048, 512)))

        _pairs = [(dc, tc) for tc in (3, 2, 1, 0) for dc in range(8)]
        _ready = {3: 1, 2: 3, 1: 5, 0: 7}
        sched = {s: [] for s in range(8)}
        _s = 1
        for dc, tc in _pairs:
            while _s < 7 and (len(sched[_s]) >= 3 or _ready[tc] > _s):
                _s += 1
            if _ready[tc] <= _s and len(sched[_s]) < 3:
                sched[_s].append((dc, tc))
        _late = [p for p in _pairs if not any(p in v for v in sched.values())]

        for q4 in range(8):
            if q4 == 1:
                wload(rwq, d["i_wq"], 8, 1024, b_wq)
                wload(rwqs, d["i_wqs"], 8, 1024, b_wqs)
            for j in range(4):
                st = q4 * 4 + j
                sl = st % 6
                gdma([], [b_xst[sl]], lambda st=st, sl=sl: GE.dma_start(
                    rxst.v(sl * 2048, 1024),
                    d["i_xseq"][st * 128:(st + 1) * 128, :]))
            ckpt(0.3 + q4 / 100.0)
            for c in range(8):
                ph = c % 2
                def tr_emit(q4=q4, c=c, ph=ph):
                    last = None
                    for j in range(4):
                        sl = (q4 * 4 + j) % 6
                        last = TE.transpose(
                            pT[:, ph * 512 + j * 128:ph * 512 + (j + 1) * 128],
                            rxst.v(sl * 2048 + c * 256, 128), idb_b[:, :])
                    return last
                top([b_xst[(q4 * 4 + j) % 6] for j in range(4)] + [b_idb], [bpT[ph]], tr_emit)
                ckpt(0.4 + (q4 * 8 + c) / 100.0)
                vop([bpT[ph]], [b_x0b[c]], lambda q4=q4, c=c, ph=ph: VE.tensor_copy(
                    x0bv(c, q4 * 512, 512), pT[:, ph * 512:(ph + 1) * 512]))
                ckpt(0.45 + (q4 * 8 + c) / 1000.0)
            for p in sched[q4]:
                emit_pair(*p)
            ckpt(0.5 + q4 / 100.0)

        ckpt(1)
        for p in _late:
            emit_pair(*p)
        ckpt(2)
        bar()
        rxst.free()
        rwq.free(); rwqs.free(); rcq.free(); rsq.free()

        rwd = Reg(K, "wd", 8192, BF16)
        rlat = Reg(K, "lat", 4 * 8192, BF16)
        b_wd = Buf()
        b_lat = [Buf() for _ in range(4)]
        wload(rwd, d["i_wd"], 8, 512, b_wd)

        for ldc in range(4):
            for tc in range(8):
                pj = tc % 2
                bC = bpC if pj == 0 else bpC2
                def lat_emit(ldc=ldc, tc=tc, pj=pj):
                    last = None
                    for dn in range(8):
                        last = TE.matmul(
                            pC[:, pj * 512:(pj + 1) * 512],
                            rwd.v(dn * 1024 + ldc * 256, 128),
                            x0bv(dn, tc * 512, 512), start=(dn == 0), stop=(dn == 7))
                    return last
                top([b_wd] + b_x0b, [bC], lat_emit)
                vop([bC], [b_lat[ldc]], lambda ldc=ldc, tc=tc, pj=pj: VE.tensor_copy(
                    rlat.v(ldc * 8192 + tc * 1024, 512), pC[:, pj * 512:(pj + 1) * 512]))

        rwk = Reg(K, "wk", 8192, BF16)
        rwks = Reg(K, "wks", 8192, BF16)
        rwv = Reg(K, "wv", 8192, BF16)
        b_wk, b_wks, b_wv = Buf(), Buf(), Buf()
        wload(rwk, d["i_wk"], 4, 1024, b_wk)
        wload(rwks, d["i_wks"], 4, 1024, b_wks)
        wload(rwv, d["i_wv"], 4, 1024, b_wv)
        rck = Reg(K, "ck", 16384, F32)
        rsk = Reg(K, "sk", 16384, F32)
        b_ck, b_sk = Buf(), Buf()
        hdma([], [b_ck], lambda: SY.dma_start(rck.v(0, 4096), d["i_ck"]))
        hdma([], [b_sk], lambda: SY.dma_start(rsk.v(0, 4096), d["i_sk"]))
        ckpt(3)
        bar()

        rwd.free()
        kgo = 0
        aoo = 2 * 8192
        K.A.free(rx0b.off + 6 * 8192, 16384)
        rvg = Reg(K, "vg", 16640, BF16)
        raos = Reg(K, "aos", 2048, BF16)
        rpe = Reg(K, "pe", 2 * 2048, BF16)
        rpo = Reg(K, "po", 2 * 2048, BF16)
        rden = Reg(K, "den", 2 * 2048, F32)
        b_kg = [Buf(), Buf()]
        b_ao, b_aos, b_vg = Buf(), Buf(), Buf()
        b_pe = [Buf(), Buf()]
        b_po = [Buf(), Buf()]
        b_den = [Buf(), Buf()]

        def kgv(j, c0, w, p0=0, p1=128):
            return rx0b.v(kgo + j * 8192 + c0 * 2, w, p0, p1)

        def aov(boff, cols, p0=0, p1=128):
            return rx0b.v(aoo + boff, cols, p0, p1)

        for g in range(4):
            for j, dc in enumerate((2 * g, 2 * g + 1)):
                for tc in range(8):
                    pj = tc % 2
                    bA = bpA if pj == 0 else bpA2
                    bB = bpB if pj == 0 else bpB2
                    def k_emit(dc=dc, tc=tc, pj=pj, reg=rwk, ps=pA):
                        last = None
                        for ldc in range(4):
                            last = TE.matmul(
                                ps[:, pj * 512:(pj + 1) * 512],
                                reg.v(ldc * 2048 + dc * 256, 128),
                                rlat.v(ldc * 8192 + tc * 1024, 512),
                                start=(ldc == 0), stop=(ldc == 3))
                        return last
                    top([b_wk] + b_lat, [bA], lambda f=k_emit: f())
                    top([b_wks] + b_lat, [bB],
                        lambda dc=dc, tc=tc, pj=pj: k_emit(dc, tc, pj, rwks, pB))
                    vop([bA, b_ck], [b_rt[0]], lambda tc=tc, pj=pj: VE.tensor_mul(
                        rrt.v(0, 512), pA[:, pj * 512:(pj + 1) * 512],
                        rck.v(tc * 2048, 512)))
                    vop([bB, b_sk], [b_rt[1]], lambda tc=tc, pj=pj: VE.tensor_mul(
                        rrt.v(2048, 512), pB[:, pj * 512:(pj + 1) * 512],
                        rsk.v(tc * 2048, 512)))
                    vop([b_rt[0], b_rt[1]], [b_kg[j]], lambda j=j, tc=tc: VE.tensor_add(
                        kgv(j, tc * 512, 512), rrt.v(0, 512), rrt.v(2048, 512)))
            for tt in range(32):
                pj = tt % 2
                ps = pD if pj == 0 else pC
                bps = bpD if pj == 0 else bpC
                def v_emit(tt=tt, g=g, ps=ps):
                    last = None
                    for ldc in range(4):
                        last = TE.matmul(
                            ps[:, 0:256],
                            rlat.v(ldc * 8192 + tt * 256, 128),
                            rwv.v(ldc * 2048 + g * 512, 256),
                            start=(ldc == 0), stop=(ldc == 3))
                    return last
                top([b_wv] + b_lat, [bps], v_emit)
                vop([], [b_vg], lambda tt=tt: VE.memset(
                    rvg.ap(tt * 520 + 128, [[65, 4], [1, 1]]), 1.0))
                vop([bps, b_vg], [b_vg], lambda tt=tt, ps=ps: VE.tensor_copy(
                    rvg.ap(tt * 520, [[65, 4], [1, 64]]),
                    AP(ps, 0, [[ps.shape[1], 128], [64, 4], [1, 64]])))
            for j, dc in enumerate((2 * g, 2 * g + 1)):
                def epilogue(qg, dc=dc):
                    for p in (0, 1):
                        obuf = bpC if p == 0 else bpC2
                        ooff = 0 if p == 0 else 512
                        dof = p * 2048
                        sop([obuf], [b_den[p]], lambda ooff=ooff, dof=dof: SE.copy(
                            rden.v(dof, 512, 0, 65), pC[0:65, ooff:ooff + 512]))
                        sop([b_den[p]], [b_den[p]], lambda dof=dof: SE.activation(
                            rden.v(dof, 512, 64, 65), rden.v(dof, 512, 64, 65), AF.Ln))
                        sop([b_den[p]], [b_den[p]], lambda dof=dof: SE.activation(
                            rden.v(dof, 512, 64, 65), rden.v(dof, 512, 64, 65),
                            AF.Exp, scale=-1.0))
                        top([b_den[p], b_ones], [bpD], lambda dof=dof: TE.matmul(
                            pD[0:64, 0:512], ones_f[64:65, 0:64],
                            rden.v(dof, 512, 64, 65), start=True, stop=True))
                        if p == 0:
                            vop([b_den[p], bpD], [b_ao], lambda dc=dc, qg=qg, dof=dof:
                                VE.tensor_mul(
                                    aov(dc * 4096 + qg * 1024, 512, 0, 64),
                                    rden.v(dof, 512, 0, 64), pD[0:64, 0:512]))
                        else:
                            vop([b_den[p], bpD], [b_aos], lambda qg=qg, dof=dof:
                                VE.tensor_mul(
                                    raos.v((qg % 2) * 1024, 512, 0, 64),
                                    rden.v(dof, 512, 0, 64), pD[0:64, 0:512]))
                    if qg % 2 == 1:
                        gdma([b_aos], [b_ao], lambda dc=dc, qg=qg: GE.dma_start(
                            aov(dc * 4096 + (qg - 1) * 1024, 1024, 64, 128),
                            raos.v(0, 1024, 0, 64)))

                def emit_av(prev):
                    qg, c0, n0, n1, sl, last = prev
                    nch = ext_chunks(4 * qg)
                    for p in (0, 1):
                        rp = rpe if p == 0 else rpo
                        bP = b_pe[sl] if p == 0 else b_po[sl]
                        obuf = bpC if p == 0 else bpC2
                        ooff = 0 if p == 0 else 512
                        def av1(g=g, j=j, p=p, c0=c0, n0=n0, n1=n1, sl=sl,
                                rp=rp, ooff=ooff, nch=nch):
                            hh = j * 2 + p
                            lastins = None
                            for ci, (off, w) in enumerate(((0, n0), (n0, n1))):
                                if w <= 0:
                                    continue
                                c = c0 + ci
                                lastins = TE.matmul(
                                    pC[0:65, ooff:ooff + w],
                                    rvg.v(c * 520 + hh * 130, 65),
                                    rp.v(sl * 2048 + off * 2, w),
                                    start=(c == 0), stop=(c == nch - 1))
                            return lastins
                        top([bP, b_vg], [obuf], av1)
                    if last:
                        epilogue(qg)

                prev = None
                sl = 0
                for qg in range(4):
                    nch = ext_chunks(4 * qg)
                    for c0 in range(0, nch, 2):
                        na = [128 * sum(1 for i in range(4 * qg, 4 * qg + 4)
                                        if ext_chunks(i) > c) for c in (c0, c0 + 1)]
                        n0, n1 = na
                        segs = [(0, n0, c0)]
                        if n0 >= 512 or n0 + n1 <= 512:
                            segs.append((n0, n1, c0 + 1))
                        else:
                            segs.append((n0, 512 - n0, c0 + 1))
                            segs.append((512, n0 + n1 - 512, c0 + 1))
                        for p in (0, 1):
                            S = pA if p == 0 else pB
                            bS = [bpA, bpA2] if p == 0 else [bpB, bpB2]
                            def sc_emit(j=j, dc=dc, qg=qg, p=p, S=S,
                                        segs=tuple(segs), n0=n0, c0=c0):
                                lastins = None
                                for off, w, c in segs:
                                    if w <= 0:
                                        continue
                                    qs = off if c == c0 else off - n0
                                    lastins = TE.matmul(
                                        S[:, off:off + w],
                                        kgv(j, c * 128, 128, p * 64, (p + 1) * 64),
                                        qv(dc, qg * 512 + qs, w, p * 64, (p + 1) * 64),
                                        start=True, stop=True)
                                return lastins
                            top([b_kg[j], b_q[dc]], bS, sc_emit)
                        for p in (0, 1):
                            S = pA if p == 0 else pB
                            bS = [bpA, bpA2] if p == 0 else [bpB, bpB2]
                            rp = rpe if p == 0 else rpo
                            bP = b_pe[sl] if p == 0 else b_po[sl]
                            sop(bS, [bP], lambda S=S, n=n0 + n1, rp=rp, sl=sl:
                                SE.activation(rp.v(sl * 2048, n), S[:, 0:n], AF.Exp))
                        if prev is not None:
                            emit_av(prev)
                        prev = (qg, c0, n0, n1, sl, c0 + 2 >= nch)
                        sl = 1 - sl
                emit_av(prev)
            ckpt(5 if g == 0 else 6)
            bar()

        rvg.free(); raos.free(); rpe.free(); rpo.free(); rden.free()
        rlat.free(); rq.free(); rrt.free()
        rwk.free(); rwks.free(); rwv.free(); rck.free(); rsk.free()

        rwo = Reg(K, "wo", 16384, BF16)
        b_wo = Buf()
        wload(rwo, d["i_wo"], 8, 1024, b_wo)
        rx1 = Reg(K, "x1", 16 * 4096, F32)
        rxo = Reg(K, "xor", 2 * 4096, F32)
        b_x1 = [Buf() for _ in range(16)]
        b_xor = [Buf(), Buf()]

        def x1v(tt, c0, w):
            return rx1.v(tt * 4096 + c0 * 4, w)

        def xo_dma(tt):
            sl = tt % 2
            hdma([], [b_xor[sl]], lambda tt=tt, sl=sl: SY.dma_start(
                rxo.v(sl * 4096, 1024), d["i_xown"][tt * 128:(tt + 1) * 128, :]))

        xo_dma(0)
        xo_dma(1)
        for tt in range(16):
            sl = tt % 2
            for dh in range(2):
                bA = bpA if dh == 0 else bpA2
                def op_emit(tt=tt, dh=dh):
                    last = None
                    for dc in range(8):
                        last = TE.matmul(
                            pA[:, dh * 512:(dh + 1) * 512],
                            aov(dc * 4096 + tt * 256, 128),
                            rwo.v(dc * 2048 + dh * 1024, 512),
                            start=(dc == 0), stop=(dc == 7))
                    return last
                top([b_ao, b_wo], [bA], op_emit)
                vop([bA, b_xor[sl]], [b_x1[tt]], lambda tt=tt, sl=sl, dh=dh:
                    VE.tensor_add(
                        x1v(tt, dh * 512, 512), pA[:, dh * 512:(dh + 1) * 512],
                        rxo.v(sl * 4096 + dh * 2048, 512)))
            if tt + 2 < 16:
                xo_dma(tt + 2)
            ln_tile(K, x1v, tt, sq_scr, lnt, eps_t, b_lnt, b_eps, b_x1[tt])
        ckpt(7)
        bar()
        K.A.free(rx0b.off, 6 * 8192)
        rxo.free()

        rx1b = Reg(K, "x1b", 8 * 2048, BF16)
        rxlo = Reg(K, "xlo", 8 * 1024, BF16)
        ru = Reg(K, "u", 6 * 2048, BF16)
        rw1 = [Reg(K, "w1a", 24576, BF16), Reg(K, "w1b", 24576, BF16)]
        rw2 = Reg(K, "w2", 12288, BF16)
        rgwh = Reg(K, "gwh", 128, BF16)
        rgwl = Reg(K, "gwl", 128, BF16)
        rxbtm = Reg(K, "xbtm", 2 * 2048, BF16)
        rgwtf = Reg(K, "gwtf", 256, F32)
        rsg = Reg(K, "sg", 2 * 2048, F32)
        rtmp = Reg(K, "tmp", 2 * 2048, F32)
        rwcol = Reg(K, "wcol", 512, F32)
        rgst = Reg(K, "gst", 2048, F32)
        rgsc = Reg(K, "gsc", 64, F32)
        rout = Reg(K, "out", 2 * 4096, F32)
        b_x1b, b_xlo, b_u = Buf(), Buf(), Buf()
        b_w1 = [Buf(), Buf()]
        b_w2 = Buf()
        b_gwh, b_gwl, b_gwtf, b_gst, b_gsc = Buf(), Buf(), Buf(), Buf(), Buf()
        b_xbtm = [Buf(), Buf()]
        b_sg = [Buf(), Buf()]
        b_tmp = [Buf(), Buf()]
        b_w = [Buf() for _ in range(16)]
        b_out = [Buf(), Buf()]

        hdma([], [b_gwtf], lambda: SY.dma_start(
            rgwtf.ap(0, [[8, 8], [1, 8]]),
            AP(d["i_gwt"].tensor, 0, [[8, 128], [8 * 128, 8], [1, 8]])))
        vop([b_gwtf], [b_gwh], lambda: VE.tensor_copy(
            rgwh.v(0, 64), rgwtf.v(0, 64)))
        vop([b_gwtf, b_gwh], [b_gwl], lambda: VE.tensor_sub(
            rgwl.v(0, 64), rgwtf.v(0, 64), rgwh.v(0, 64)))

        def w1_dma(slot, ei):
            if ei < 8:
                iap = AP(d["i_w1t"].tensor, ei * 1024 * 1536,
                         [[1536, 128], [1536 * 128, 8], [1, 1536]])
            else:
                iap = AP(d["i_ws1t"].tensor, (ei - 8) * 1536,
                         [[3072, 128], [3072 * 128, 8], [1, 1536]])
            oap = rw1[slot].ap(0, [[1536, 8], [1, 1536]])
            gdma([], [b_w1[slot]], lambda oap=oap, iap=iap: GE.dma_start(oap, iap))

        def w2_dma(ei):
            if ei < 8:
                iap = AP(d["i_w2t"].tensor, ei * 768 * 1024,
                         [[1024, 128], [1024 * 128, 6], [1, 1024]])
            else:
                iap = AP(d["i_ws2t"].tensor, (ei - 8) * 768 * 1024,
                         [[1024, 128], [1024 * 128, 6], [1, 1024]])
            oap = rw2.ap(0, [[1024, 6], [1, 1024]])
            gdma([], [b_w2], lambda oap=oap, iap=iap: GE.dma_start(oap, iap))

        w1_dma(0, 0)
        w2_dma(0)

        for hf in range(2):
            for tl in range(8):
                tt = hf * 8 + tl
                sl = tl % 2
                vop([b_x1[tt]], [b_xbtm[sl]], lambda tt=tt, sl=sl: VE.tensor_copy(
                    rxbtm.v(sl * 2048, 1024), x1v(tt, 0, 1024)))
                def tx_emit(sl=sl):
                    last = None
                    for c in range(8):
                        last = TE.transpose(
                            pT[:, c * 128:(c + 1) * 128],
                            rxbtm.v(sl * 2048 + c * 256, 128), idb_b[:, :])
                    return last
                top([b_xbtm[sl], b_idb], bpT, tx_emit)
                vop(bpT, [b_x1b], lambda tl=tl: VE.tensor_copy(
                    rx1b.ap(tl * 256, [[1024, 8], [1, 128]]),
                    AP(pT, 0, [[1024, 128], [128, 8], [1, 128]])))
                vop([b_x1[tt], b_xbtm[sl]], [b_xbtm[sl]], lambda tt=tt, sl=sl:
                    VE.tensor_sub(
                        rxbtm.v(sl * 2048, 1024), x1v(tt, 0, 1024),
                        rxbtm.v(sl * 2048, 1024)))
                def tlx_emit(sl=sl):
                    last = None
                    for c in range(8):
                        last = TE.transpose(
                            pT[:, c * 128:(c + 1) * 128],
                            rxbtm.v(sl * 2048 + c * 256, 128), idb_b[:, :])
                    return last
                top([b_xbtm[sl], b_idb], bpT, tlx_emit)
                vop(bpT, [b_xlo], lambda tl=tl: VE.tensor_copy(
                    rxlo.ap((tl % 4) * 256, [[512, 8], [1, 128]]),
                    AP(pT, 0, [[1024, 128], [128, 8], [1, 128]])))
                if tl % 4 == 3:
                    tc = tl // 4
                    def g_emit(tc=tc):
                        last = None
                        k = 0
                        for reg, isxlo in ((rgwh, False), (rgwl, False), (rgwh, True)):
                            for dn in range(8):
                                xap = (rxlo.v(dn * 1024, 512) if isxlo
                                       else rx1b.v(dn * 2048 + tc * 1024, 512))
                                last = TE.matmul(
                                    pD[0:8, 0:512], reg.v(dn * 16, 8), xap,
                                    start=(k == 0), stop=(k == 23))
                                k += 1
                        return last
                    top([b_gwh, b_gwl, b_x1b, b_xlo], [bpD], g_emit)
                    vop([bpD], [b_gst], lambda: VE.tensor_copy(
                        rgst.v(0, 512, 0, 8), pD[0:8, 0:512]))
                    def gt_emit():
                        last = None
                        for jj in range(4):
                            last = TE.transpose(
                                pC[:, jj * 8:(jj + 1) * 8],
                                rgst.v(jj * 512, 128, 0, 8), idb_f[0:8, 0:8])
                        return last
                    top([b_gst, b_idb], [bpC], gt_emit)
                    for jj in range(4):
                        ti = hf * 8 + tc * 4 + jj
                        gate_softmax_top2(K, pC, jj, rwcol, rgsc, ti, lnt,
                                          bpC, b_w[ti], b_gsc, b_lng)
            ckpt(8 if hf == 0 else 10)
            for ei in range(10):
                slot = ei % 2
                nxt = ei + 1 if ei < 9 else (0 if hf == 0 else None)
                if nxt is not None:
                    w1_dma(1 - slot, nxt)
                for m in range(6):
                    for tc in range(2):
                        pj = (m * 2 + tc) % 2
                        ps = pA if pj == 0 else pB
                        bY = bpA if pj == 0 else bpB
                        bG = bpA2 if pj == 0 else bpB2
                        def w1_emit(m=m, tc=tc, slot=slot, ps=ps, blk=0):
                            last = None
                            for dn in range(8):
                                last = TE.matmul(
                                    ps[:, blk * 512:(blk + 1) * 512],
                                    rw1[slot].v(dn * 3072 + (2 * m + blk) * 256, 128),
                                    rx1b.v(dn * 2048 + tc * 1024, 512),
                                    start=(dn == 0), stop=(dn == 7))
                            return last
                        top([b_w1[slot], b_x1b], [bY], lambda f=w1_emit: f())
                        top([b_w1[slot], b_x1b], [bG],
                            lambda m=m, tc=tc, slot=slot, ps=ps: w1_emit(m, tc, slot, ps, 1))
                        ssl = (m * 2 + tc) % 2
                        sop([bG], [b_sg[ssl]], lambda ps=ps, ssl=ssl: SE.activation(
                            rsg.v(ssl * 2048, 512), ps[:, 512:1024], AF.Silu))
                        vop([bY, b_sg[ssl]], [b_u], lambda m=m, tc=tc, ps=ps, ssl=ssl:
                            VE.tensor_mul(
                                ru.v(m * 2048 + tc * 1024, 512),
                                ps[:, 0:512], rsg.v(ssl * 2048, 512)))
                for tl in range(8):
                    tt = hf * 8 + tl
                    for dh in range(2):
                        pj = (tl * 2 + dh) % 2
                        bZ = bpC if pj == 0 else bpC2
                        zo = 0 if pj == 0 else 512
                        def w2_emit(tl=tl, dh=dh, zo=zo):
                            last = None
                            for m in range(6):
                                last = TE.matmul(
                                    pC[:, zo:zo + 512],
                                    ru.v(m * 2048 + tl * 256, 128),
                                    rw2.v(m * 2048 + dh * 1024, 512),
                                    start=(m == 0), stop=(m == 5))
                            return last
                        top([b_w2, b_u], [bZ], w2_emit)
                        if ei < 8:
                            ssl = (tl * 2 + dh) % 2
                            sop([bZ, b_w[tt]], [b_tmp[ssl]],
                                lambda zo=zo, ssl=ssl, tt=tt, ei=ei: SE.activation(
                                    rtmp.v(ssl * 2048, 512),
                                    pC[:, zo:zo + 512], AF.Identity,
                                    scale=rwcol.v(tt * 32 + ei * 4, 1)))
                            vop([b_tmp[ssl], b_x1[tt]], [b_x1[tt]],
                                lambda tt=tt, dh=dh, ssl=ssl: VE.tensor_add(
                                    x1v(tt, dh * 512, 512), x1v(tt, dh * 512, 512),
                                    rtmp.v(ssl * 2048, 512)))
                        else:
                            vop([bZ, b_x1[tt]], [b_x1[tt]],
                                lambda tt=tt, dh=dh, zo=zo: VE.tensor_add(
                                    x1v(tt, dh * 512, 512), pC[:, zo:zo + 512],
                                    x1v(tt, dh * 512, 512)))
                    if ei == 9:
                        sl = tl % 2
                        ln_tile(K, x1v, tt, sq_scr, lnt, eps_t, b_lnt, b_eps,
                                b_x1[tt], out=(rout.v(sl * 4096, 1024), b_out[sl]))
                        hdma([b_out[sl]], [], lambda tt=tt, sl=sl: SY.dma_start(
                            y[tt * 128:(tt + 1) * 128, :], rout.v(sl * 4096, 1024)))
                if nxt is not None:
                    w2_dma(nxt)
                ckpt(9 if (hf == 0 and ei == 0) else 99)

        ckpt(98)

    with nc.Block():
        try:
            _body()
        except StopBuild:
            pass
        for i in range(4):
            SY.wait_ge(K.sems[f"dh{i}"], K.cnt[f"dh{i}"])
            GE.wait_ge(K.sems[f"dg{i}"], K.cnt[f"dg{i}"])

    for cm in K.sem_ctx:
        cm.__exit__(None, None, None)
    return nc


def ln_tile(K, x1v, tt, sq_scr, lnt, eps_t, b_lnt_l, b_eps, b_x1, out=None):
    SE, VE = K.nc.scalar, K.nc.vector
    vop, sop = K.vop, K.sop
    b_lnt = b_lnt_l[tt % 2]
    o = (tt % 2) * 16

    def c(i):
        return lnt[:, o + i:o + i + 1]
    vop([b_x1], [b_lnt], lambda: VE.reduce_sum(c(0), x1v(tt, 0, 1024), axis=AX.X))
    sop([b_x1], [b_lnt], lambda: SE.activation(
        sq_scr[:, :], x1v(tt, 0, 1024), AF.Square, accum_out=c(1)))
    vop([b_lnt], [b_lnt], lambda: VE.tensor_scalar_mul(c(2), c(0), 1.0 / 1024.0))
    vop([b_lnt], [b_lnt], lambda: VE.tensor_scalar_mul(c(3), c(1), 1.0 / 1024.0))
    vop([b_lnt], [b_lnt], lambda: VE.tensor_mul(c(4), c(2), c(2)))
    vop([b_lnt], [b_lnt], lambda: VE.tensor_sub(c(5), c(3), c(4)))
    sop([b_lnt, b_eps], [b_lnt], lambda: SE.activation(
        c(6), c(5), AF.Ln, bias=eps_t[:, 0:1]))
    sop([b_lnt], [b_lnt], lambda: SE.activation(c(7), c(6), AF.Exp, scale=-0.5))
    vop([b_lnt], [b_lnt], lambda: K.nc.vector.scalar_tensor_tensor(
        c(8), c(2), -1.0, c(7), ALU.mult, ALU.mult))
    if out is None:
        sop([b_lnt], [b_x1], lambda: SE.activation(
            x1v(tt, 0, 1024), x1v(tt, 0, 1024), AF.Identity,
            bias=c(8), scale=c(7)))
    else:
        oap, b_o = out
        sop([b_lnt, b_x1], [b_o], lambda: SE.activation(
            oap, x1v(tt, 0, 1024), AF.Identity,
            bias=c(8), scale=c(7)))


def gate_softmax_top2(K, pC, jj, rwcol, rgsc, ti, lnt, bpC, b_w, b_gsc, b_lnt):
    VE, SE = K.nc.vector, K.nc.scalar
    vop, sop = K.vop, K.sop
    wv = lambda: rwcol.v(ti * 32, 8)
    gs = lambda: rgsc.v(0, 8)

    def c(i):
        return lnt[:, 32 + i:32 + i + 1]
    vop([bpC], [b_w], lambda: VE.tensor_copy(wv(), pC[:, jj * 8:(jj + 1) * 8]))
    vop([b_w], [b_lnt], lambda: VE.reduce_max(c(0), wv(), axis=AX.X))
    vop([b_w, b_lnt], [b_w], lambda: VE.tensor_scalar(
        wv(), wv(), c(0), None, ALU.subtract))
    sop([b_w], [b_w], lambda: SE.activation(wv(), wv(), AF.Exp))
    vop([b_w], [b_lnt], lambda: VE.reduce_sum(c(1), wv(), axis=AX.X))
    vop([b_lnt], [b_lnt], lambda: VE.reciprocal(c(2), c(1)))
    vop([b_w, b_lnt], [b_w], lambda: VE.tensor_scalar(
        wv(), wv(), c(2), None, ALU.mult))
    vop([b_w], [b_lnt], lambda: VE.reduce_max(c(3), wv(), axis=AX.X))
    vop([b_w, b_lnt], [b_gsc], lambda: VE.tensor_scalar(
        gs(), wv(), c(3), None, ALU.is_equal))
    vop([b_gsc, b_w], [b_gsc], lambda: VE.scalar_tensor_tensor(
        gs(), gs(), -1e30, wv(), ALU.mult, ALU.add))
    vop([b_gsc], [b_lnt], lambda: VE.reduce_max(c(4), gs(), axis=AX.X))
    vop([b_w, b_lnt], [b_gsc], lambda: VE.tensor_scalar(
        gs(), wv(), c(4), None, ALU.is_ge))
    vop([b_gsc, b_w], [b_w], lambda: VE.tensor_mul(wv(), wv(), gs()))


_CACHED_NC = None


def _get_nc():
    global _CACHED_NC
    if _CACHED_NC is None:
        _CACHED_NC = build_program()
    return _CACHED_NC


def _perm_deint():
    p = np.zeros(E, np.int64)
    for h in range(16):
        for jx in range(64):
            p[h * 64 + jx] = h * 64 + (2 * jx if jx < 32 else 2 * (jx - 32) + 1)
    return p


def _perm_swap():
    p = np.zeros(E, np.int64)
    for h in range(16):
        for jx in range(64):
            p[h * 64 + jx] = h * 64 + (jx + 32 if jx < 32 else jx - 32)
    return p


def kernel(L0_attn_x, L0_attn_wq, L0_attn_w_down, L0_attn_wk_up, L0_attn_wv_up,
           L0_attn_wo, L0_ffn_gate_w, L0_ffn_W1, L0_ffn_W2, L0_ffn_Ws1,
           L0_ffn_Ws2):
    x = np.asarray(L0_attn_x, np.float32)
    wq = np.asarray(L0_attn_wq, np.float32)
    wd = np.asarray(L0_attn_w_down, np.float32)
    wk = np.asarray(L0_attn_wk_up, np.float32)
    wv = np.asarray(L0_attn_wv_up, np.float32)
    wo = np.asarray(L0_attn_wo, np.float32)
    gw = np.asarray(L0_ffn_gate_w, np.float32)
    W1 = np.asarray(L0_ffn_W1, np.float32)
    W2 = np.asarray(L0_ffn_W2, np.float32)
    Ws1 = np.asarray(L0_ffn_Ws1, np.float32)
    Ws2 = np.asarray(L0_ffn_Ws2, np.float32)

    pd_ = _perm_deint()
    psw = _perm_swap()
    C = np.ascontiguousarray
    wq_p = C(wq[:, pd_])
    wq_s = C(wq_p[:, psw])
    wk_p = C(wk[:, pd_])
    wk_s = C(wk_p[:, psw])

    ro1 = np.concatenate([np.r_[j * 128:(j + 1) * 128, 768 + j * 128:768 + (j + 1) * 128]
                          for j in range(6)])
    W1t = C(np.concatenate([W1[e][ro1].T for e in range(8)], axis=0))
    W2t = C(np.concatenate([W2[e].T for e in range(8)], axis=0))
    ro2 = np.concatenate([np.r_[j * 128:(j + 1) * 128, 1536 + j * 128:1536 + (j + 1) * 128]
                          for j in range(12)])
    Ws1t = C(Ws1[ro2].T)
    Ws2t = C(Ws2.T)
    gwt = C(gw.T)
    idn = np.eye(128, dtype=np.float32)

    inv = 1.0 / (10000.0 ** (np.arange(0, 64, 2, dtype=np.float32) / 64))
    ang = np.outer(np.arange(T, dtype=np.float32), inv)
    cosf = np.cos(ang).astype(np.float32)
    sinf = np.sin(ang).astype(np.float32)
    rsel = np.arange(128) % 32
    sgn = np.where((np.arange(128) % 64) < 32, -1.0, 1.0).astype(np.float32)[:, None]

    shared = dict(i_wq=wq_p, i_wqs=wq_s, i_wd=wd, i_wk=wk_p, i_wks=wk_s,
                  i_wv=wv, i_wo=wo, i_gwt=gwt, i_w1t=W1t, i_w2t=W2t,
                  i_ws1t=Ws1t, i_ws2t=Ws2t, i_idn=idn)

    in_maps = []
    meta = []
    for core in range(8):
        seq, half = core // 2, core % 2
        idx_r = np.zeros(T, np.int64)
        for p in range(16):
            idx_r[p * 256:p * 256 + 128] = p * 256 + half * 128 + np.arange(128)
            idx_r[p * 256 + 128:(p + 1) * 256] = p * 256 + (1 - half) * 128 + np.arange(128)
        idx_own = np.concatenate([(15 - i) * 256 + np.arange(128) for i in range(16)])
        own_pos = idx_r[idx_own]
        xr = C(x[seq][idx_r])
        m = dict(shared)
        m.update(i_xseq=xr, i_xown=C(xr[idx_own]),
                 i_ck=C(cosf[idx_r].T[rsel]),
                 i_sk=C(sinf[idx_r].T[rsel] * sgn),
                 i_cq=C(0.125 * cosf[own_pos].T[rsel]),
                 i_sq=C(0.125 * sinf[own_pos].T[rsel] * sgn))
        in_maps.append(m)
        meta.append((seq, own_pos))

    nc = _get_nc()
    res = run_bass_kernel_spmd(nc, in_maps, core_ids=list(range(8)))

    ys = [np.asarray(res.results[core]["y"]) for core in range(8)]
    odt = np.float64 if ys[0].dtype == np.float64 else np.float32
    out = np.zeros((4, T, E), odt)
    for core in range(8):
        seq, pos = meta[core]
        out[seq, pos, :] = ys[core]
    return out


# version 25
# speedup vs baseline: 1.0535x; 1.0060x over previous
import numpy as np
import ml_dtypes

import concourse.bass as bass
import concourse.mybir as mybir
from concourse.bass import AP
from concourse.bass_utils import run_bass_kernel_spmd

F32 = mybir.dt.float32
BF16 = mybir.dt.bfloat16
AF = mybir.ActivationFunctionType
ALU = mybir.AluOpType
AX = mybir.AxisListType

T = 4096
E = 1024
LATD = 512
EPS = 1e-5
AR_BYTES = 209536


def ext_chunks(slot):
    return 2 * (16 - slot)


class Alloc:
    def __init__(self, nbytes):
        self.free_list = [[0, nbytes]]

    def alloc(self, n, align=64):
        for i, (a, b) in enumerate(self.free_list):
            s = (a + align - 1) // align * align
            if s + n <= b:
                self.free_list.pop(i)
                if a < s:
                    self.free_list.append([a, s])
                if s + n < b:
                    self.free_list.append([s + n, b])
                self.free_list.sort()
                return s
        raise RuntimeError(f"arena OOM: need {n}B, free={self.free_list}")

    def free(self, off, n):
        self.free_list.append([off, off + n])
        self.free_list.sort()
        merged = []
        for a, b in self.free_list:
            if merged and merged[-1][1] >= a:
                merged[-1][1] = max(merged[-1][1], b)
            else:
                merged.append([a, b])
        self.free_list = merged


class Buf:
    __slots__ = ("wr", "rds")

    def __init__(self):
        self.wr = None
        self.rds = []


class Reg:

    def __init__(self, K, name, nbytes, dtype):
        self.K = K
        self.nb = nbytes
        self.esz = 2 if dtype == BF16 else 4
        self.w = nbytes // self.esz
        self.off = K.A.alloc(nbytes)
        self.t = K.place(name, [128, self.w], dtype, self.off)

    def v(self, boff, cols, p0=0, p1=128):
        c0 = boff // self.esz
        return self.t[p0:p1, c0:c0 + cols]

    def ap(self, boff, dims):
        return AP(self.t, boff // self.esz, [[self.w, 128]] + dims)

    def free(self):
        self.K.A.free(self.off, self.nb)


class KB:
    def __init__(self):
        nc = bass.Bass("TRN2", target_bir_lowering=False)
        self.nc = nc
        self.cnt = {}
        self.waited = {}
        self.sem_ctx = []
        self.sems = {}
        self.A = Alloc(AR_BYTES)
        self.nreg = 0
        names = ["t", "v", "s", "g"]
        names += [f"dg{i}" for i in range(4)] + [f"dh{i}" for i in range(4)]
        for name in names:
            cm = nc.semaphore(name)
            self.sems[name] = cm.__enter__()
            self.sem_ctx.append(cm)
            self.cnt[name] = 0
        self.lane = {"dg": 0, "dh": 0}

    def init_arena(self):
        self.back = self.nc.alloc_sbuf_tensor("arena", [128, AR_BYTES], mybir.dt.uint8)
        self.base = self.nc.lookup_mloc(self.back).addr

    def place(self, name, shape, dtype, boff):
        self.nreg += 1
        h = self.nc.virtual_sbuf_tensor(f"{name}{self.nreg}", list(shape), dtype)
        self.nc._tensor(h.name, list(h.shape), dtype, type="SB")
        m = self.nc.lookup_mloc(h)
        m.allocated = True
        m.addr = self.base + boff
        return h

    def _wait(self, eng, en, tok):
        if tok is None:
            return
        s, v = tok
        if self.waited.get((en, s), 0) < v:
            eng.wait_ge(self.sems[s], v)
            self.waited[(en, s)] = v

    def _op(self, eng, en, sem, ins, outs, emit, n=1):
        for b in ins:
            self._wait(eng, en, b.wr)
        for b in outs:
            self._wait(eng, en, b.wr)
            for r in b.rds:
                self._wait(eng, en, r)
        instr = emit()
        self.cnt[sem] += n
        instr.then_inc(self.sems[sem], n)
        tok = (sem, self.cnt[sem])
        for b in ins:
            b.rds.append(tok)
        for b in outs:
            b.wr = tok
            b.rds = []
        return tok

    def top(self, ins, outs, emit):
        return self._op(self.nc.tensor, "t", "t", ins, outs, emit)

    def vop(self, ins, outs, emit):
        return self._op(self.nc.vector, "v", "v", ins, outs, emit)

    def sop(self, ins, outs, emit):
        return self._op(self.nc.scalar, "s", "s", ins, outs, emit)

    def _dma(self, eng, en, q, ins, outs, emit):
        sem = f"{q}{self.lane[q]}"
        self.lane[q] = (self.lane[q] + 1) % 4
        if self.cnt[sem] > 0:
            self._wait(eng, en, (sem, self.cnt[sem]))
        return self._op(eng, en, sem, ins, outs, emit, 16)

    def gdma(self, ins, outs, emit):
        return self._dma(self.nc.gpsimd, "g", "dg", ins, outs, emit)

    def hdma(self, ins, outs, emit):
        return self._dma(self.nc.sync, "dh", "dh", ins, outs, emit)

    def bar(self):
        signal = [("t", self.nc.tensor), ("v", self.nc.vector),
                  ("s", self.nc.scalar), ("g", self.nc.gpsimd)]
        toks = []
        for en, eng in signal:
            self._wait(eng, en, (en, self.cnt[en]))
            self.cnt[en] += 1
            eng.nop().then_inc(self.sems[en], 1)
            toks.append((en, self.cnt[en]))
        for en, eng in signal + [("dh", self.nc.sync)]:
            for tok in toks:
                self._wait(eng, en, tok)


class StopBuild(Exception):
    pass


def build_program(limit=1000):
    def ckpt(i):
        if i >= limit:
            raise StopBuild()
    K = KB()
    nc = K.nc
    TE, VE, SE, GE, SY = nc.tensor, nc.vector, nc.scalar, nc.gpsimd, nc.sync

    d = {}
    for name, shape in (
            ("i_xseq", (T, E)), ("i_xown", (2048, E)),
            ("i_cq", (128, 2048)), ("i_sq", (128, 2048)),
            ("i_ck", (128, T)), ("i_sk", (128, T)),
            ("i_wq", (E, E)), ("i_wqs", (E, E)), ("i_wd", (E, LATD)),
            ("i_wk", (LATD, E)), ("i_wks", (LATD, E)), ("i_wv", (LATD, E)),
            ("i_wo", (E, E)), ("i_gwt", (E, 8)),
            ("i_w1t", (8 * E, 1536)), ("i_w2t", (8 * 768, E)),
            ("i_ws1t", (E, 3072)), ("i_ws2t", (1536, E)),
            ("i_idn", (128, 128))):
        d[name] = nc.dram_tensor(name, list(shape), F32, kind="ExternalInput").ap()
    y = nc.dram_tensor("y", [2048, E], F32, kind="ExternalOutput").ap()

    idb_b = nc.alloc_sbuf_tensor("idb_b", [128, 128], BF16)
    idb_f = nc.alloc_sbuf_tensor("idb_f", [128, 128], F32)
    ones_f = nc.alloc_sbuf_tensor("ones_f", [128, 64], F32)
    eps_t = nc.alloc_sbuf_tensor("eps_t", [128, 1], F32)
    lnt = nc.alloc_sbuf_tensor("lnt", [128, 48], F32)
    sq_scr = nc.alloc_sbuf_tensor("sq_scr", [128, 1024], BF16)
    K.init_arena()
    assert nc.sbuf_bytes_remaining >= 0, nc.sbuf_bytes_remaining

    pA = nc.alloc_psum_tensor("pA", [128, 1024], F32)
    pB = nc.alloc_psum_tensor("pB", [128, 1024], F32)
    pC = nc.alloc_psum_tensor("pC", [128, 1024], F32)
    pD = nc.alloc_psum_tensor("pD", [128, 512], F32)
    pT = nc.alloc_psum_tensor("pT", [128, 1024], BF16)

    top, vop, sop, gdma, hdma, bar = K.top, K.vop, K.sop, K.gdma, K.hdma, K.bar
    bpA, bpA2, bpB, bpB2, bpC, bpC2, bpD = (Buf() for _ in range(7))
    _bpT = Buf()
    bpT = [_bpT, _bpT]
    b_idb, b_ones, b_eps = Buf(), Buf(), Buf()
    b_lnt = [Buf(), Buf()]
    b_lng = Buf()

    def wload(reg, dram, nchunk, ncols, buf, boff=0):
        in_ap = AP(dram.tensor, 0, [[ncols, 128], [ncols * 128, nchunk], [1, ncols]])
        out_ap = reg.ap(boff, [[ncols, nchunk], [1, ncols]])
        return gdma([], [buf], lambda oap=out_ap, iap=in_ap: GE.dma_start(oap, iap))

    def _body():
        hdma([], [b_idb], lambda: SY.dma_start(idb_f[:, :], d["i_idn"]))
        gdma([], [b_idb], lambda: GE.dma_start(idb_b[:, :], d["i_idn"]))
        vop([], [b_ones], lambda: VE.memset(ones_f[:, :], 1.0))
        vop([], [b_eps], lambda: VE.memset(eps_t[:, :], EPS))

        rx0b = Reg(K, "x0b", 8 * 8192, BF16)
        rxst = Reg(K, "xst", 6 * 2048, BF16)
        b_x0b = [Buf() for _ in range(8)]
        b_xst = [Buf() for _ in range(6)]

        def x0bv(c, c0, w):
            return rx0b.v(c * 8192 + c0 * 2, w)

        rwq = Reg(K, "wq", 16384, BF16)
        rwqs = Reg(K, "wqs", 16384, BF16)
        b_wq, b_wqs = Buf(), Buf()
        rcq = Reg(K, "cq", 8192, F32)
        rsq = Reg(K, "sq", 8192, F32)
        b_cq, b_sq = Buf(), Buf()
        hdma([], [b_cq], lambda: SY.dma_start(rcq.v(0, 2048), d["i_cq"]))
        hdma([], [b_sq], lambda: SY.dma_start(rsq.v(0, 2048), d["i_sq"]))
        ckpt(0.2)

        rq = Reg(K, "qcol", 8 * 4096, BF16)
        b_q = [Buf() for _ in range(8)]
        rrt = Reg(K, "rt", 4096, F32)
        b_rt = [Buf(), Buf()]

        def qv(dc, c0, w, p0=0, p1=128):
            return rq.v(dc * 4096 + c0 * 2, w, p0, p1)

        def emit_pair(dc, tc):
            pj = dc % 2
            bA = bpA if pj == 0 else bpA2
            bB = bpB if pj == 0 else bpB2

            def q_emit(dc=dc, tc=tc, pj=pj, reg=rwq, ps=pA):
                last = None
                for dn in range(8):
                    rhs = rx0b.ap(dn * 8192 + (3840 - 1024 * tc) * 2,
                                  [[-256, 4], [1, 128]])
                    last = TE.matmul(
                        ps[:, pj * 512:(pj + 1) * 512],
                        reg.v(dn * 2048 + dc * 256, 128),
                        rhs, start=(dn == 0), stop=(dn == 7))
                return last
            top([b_wq] + b_x0b, [bA], lambda f=q_emit: f())
            top([b_wqs] + b_x0b, [bB],
                lambda dc=dc, tc=tc, pj=pj: q_emit(dc, tc, pj, rwqs, pB))
            vop([bA, b_cq], [b_rt[0]], lambda tc=tc, pj=pj: VE.tensor_mul(
                rrt.v(0, 512), pA[:, pj * 512:(pj + 1) * 512],
                rcq.v(tc * 2048, 512)))
            vop([bB, b_sq], [b_rt[1]], lambda tc=tc, pj=pj: VE.tensor_mul(
                rrt.v(2048, 512), pB[:, pj * 512:(pj + 1) * 512],
                rsq.v(tc * 2048, 512)))
            vop([b_rt[0], b_rt[1]], [b_q[dc]], lambda dc=dc, tc=tc: VE.tensor_add(
                qv(dc, tc * 512, 512), rrt.v(0, 512), rrt.v(2048, 512)))

        rwd = Reg(K, "wd", 8192, BF16)
        rlat = Reg(K, "lat", 4 * 8192, BF16)
        b_wd = Buf()
        b_lat = [Buf() for _ in range(4)]

        def emit_lat(ldc, tc):
            pj = (ldc + tc) % 2
            bC = bpC if pj == 0 else bpC2

            def lat_emit(ldc=ldc, tc=tc, pj=pj):
                last = None
                for dn in range(8):
                    last = TE.matmul(
                        pC[:, pj * 512:(pj + 1) * 512],
                        rwd.v(dn * 1024 + ldc * 256, 128),
                        x0bv(dn, tc * 512, 512), start=(dn == 0), stop=(dn == 7))
                return last
            top([b_wd] + b_x0b, [bC], lat_emit)
            vop([bC], [b_lat[ldc]], lambda ldc=ldc, tc=tc, pj=pj: VE.tensor_copy(
                rlat.v(ldc * 8192 + tc * 1024, 512), pC[:, pj * 512:(pj + 1) * 512]))

        _qready = {3: 3, 2: 4, 1: 6, 0: 8}
        _qpool = [(dc, tc) for tc in (3, 2, 1, 0) for dc in range(8)]
        _lpool = [(ldc, tc) for tc in range(8) for ldc in range(4)]
        _lready = {tc: max(1, tc) for tc in range(8)}

        for q4 in range(8):
            if q4 == 1:
                wload(rwd, d["i_wd"], 8, 512, b_wd)
            if q4 == 2:
                wload(rwq, d["i_wq"], 8, 1024, b_wq)
            if q4 == 3:
                wload(rwqs, d["i_wqs"], 8, 1024, b_wqs)
            for j in range(4):
                st = q4 * 4 + j
                sl = st % 6
                gdma([], [b_xst[sl]], lambda st=st, sl=sl: GE.dma_start(
                    rxst.v(sl * 2048, 1024),
                    d["i_xseq"][st * 128:(st + 1) * 128, :]))
            ckpt(0.3 + q4 / 100.0)
            for c in range(8):
                ph = c % 2
                def tr_emit(q4=q4, c=c, ph=ph):
                    last = None
                    for j in range(4):
                        sl = (q4 * 4 + j) % 6
                        last = TE.transpose(
                            pT[:, ph * 512 + j * 128:ph * 512 + (j + 1) * 128],
                            rxst.v(sl * 2048 + c * 256, 128), idb_b[:, :])
                    return last
                top([b_xst[(q4 * 4 + j) % 6] for j in range(4)] + [b_idb], [bpT[ph]], tr_emit)
                ckpt(0.4 + (q4 * 8 + c) / 100.0)
                vop([bpT[ph]], [b_x0b[c]], lambda q4=q4, c=c, ph=ph: VE.tensor_copy(
                    x0bv(c, q4 * 512, 512), pT[:, ph * 512:(ph + 1) * 512]))
                ckpt(0.45 + (q4 * 8 + c) / 1000.0)
            for ldc, tc in [p for p in _lpool if _lready[p[1]] == q4]:
                emit_lat(ldc, tc)
            nq = 0
            for p in list(_qpool):
                if _qready[p[1]] <= q4 and nq < 4:
                    emit_pair(*p)
                    _qpool.remove(p)
                    nq += 1
            ckpt(0.5 + q4 / 100.0)

        ckpt(1)
        for p in _qpool:
            emit_pair(*p)
        ckpt(2)
        bar()
        rxst.free()
        rwq.free(); rwqs.free(); rcq.free(); rsq.free()
        K.A.free(rx0b.off + 6 * 8192, 16384)

        rvg = Reg(K, "vg", 16640, BF16)
        rwk = Reg(K, "wk", 8192, BF16)
        rwks = Reg(K, "wks", 8192, BF16)
        rwv = Reg(K, "wv", 8192, BF16)
        b_wk, b_wks, b_wv = Buf(), Buf(), Buf()
        wload(rwk, d["i_wk"], 4, 1024, b_wk)
        wload(rwks, d["i_wks"], 4, 1024, b_wks)
        wload(rwv, d["i_wv"], 4, 1024, b_wv)
        rck = Reg(K, "ck", 16384, F32)
        rsk = Reg(K, "sk", 16384, F32)
        b_ck, b_sk = Buf(), Buf()
        hdma([], [b_ck], lambda: SY.dma_start(rck.v(0, 4096), d["i_ck"]))
        hdma([], [b_sk], lambda: SY.dma_start(rsk.v(0, 4096), d["i_sk"]))
        ckpt(3)
        bar()

        rwd.free()
        kgo = 0
        aoo = 2 * 8192
        raos = Reg(K, "aos", 2048, BF16)
        rpe = Reg(K, "pe", 2 * 2048, BF16)
        rpo = Reg(K, "po", 2 * 2048, BF16)
        rden = Reg(K, "den", 2 * 2048, F32)
        b_kg = [Buf(), Buf()]
        b_ao, b_aos, b_vg = Buf(), Buf(), Buf()
        b_pe = [Buf(), Buf()]
        b_po = [Buf(), Buf()]
        b_den = [Buf(), Buf()]

        def kgv(j, c0, w, p0=0, p1=128):
            return rx0b.v(kgo + j * 8192 + c0 * 2, w, p0, p1)

        def aov(boff, cols, p0=0, p1=128):
            return rx0b.v(aoo + boff, cols, p0, p1)

        for g in range(4):
            for j, dc in enumerate((2 * g, 2 * g + 1)):
                for tc in range(8):
                    pj = tc % 2
                    bA = bpA if pj == 0 else bpA2
                    bB = bpB if pj == 0 else bpB2
                    def k_emit(dc=dc, tc=tc, pj=pj, reg=rwk, ps=pA):
                        last = None
                        for ldc in range(4):
                            last = TE.matmul(
                                ps[:, pj * 512:(pj + 1) * 512],
                                reg.v(ldc * 2048 + dc * 256, 128),
                                rlat.v(ldc * 8192 + tc * 1024, 512),
                                start=(ldc == 0), stop=(ldc == 3))
                        return last
                    top([b_wk] + b_lat, [bA], lambda f=k_emit: f())
                    top([b_wks] + b_lat, [bB],
                        lambda dc=dc, tc=tc, pj=pj: k_emit(dc, tc, pj, rwks, pB))
                    vop([bA, b_ck], [b_rt[0]], lambda tc=tc, pj=pj: VE.tensor_mul(
                        rrt.v(0, 512), pA[:, pj * 512:(pj + 1) * 512],
                        rck.v(tc * 2048, 512)))
                    vop([bB, b_sk], [b_rt[1]], lambda tc=tc, pj=pj: VE.tensor_mul(
                        rrt.v(2048, 512), pB[:, pj * 512:(pj + 1) * 512],
                        rsk.v(tc * 2048, 512)))
                    vop([b_rt[0], b_rt[1]], [b_kg[j]], lambda j=j, tc=tc: VE.tensor_add(
                        kgv(j, tc * 512, 512), rrt.v(0, 512), rrt.v(2048, 512)))
            for tt in range(32):
                pj = tt % 2
                ps = pD if pj == 0 else pC
                bps = bpD if pj == 0 else bpC
                def v_emit(tt=tt, g=g, ps=ps):
                    last = None
                    for ldc in range(4):
                        last = TE.matmul(
                            ps[:, 0:256],
                            rlat.v(ldc * 8192 + tt * 256, 128),
                            rwv.v(ldc * 2048 + g * 512, 256),
                            start=(ldc == 0), stop=(ldc == 3))
                    return last
                top([b_wv] + b_lat, [bps], v_emit)
                vop([], [b_vg], lambda tt=tt: VE.memset(
                    rvg.ap(tt * 520 + 128, [[65, 4], [1, 1]]), 1.0))
                vop([bps, b_vg], [b_vg], lambda tt=tt, ps=ps: VE.tensor_copy(
                    rvg.ap(tt * 520, [[65, 4], [1, 64]]),
                    AP(ps, 0, [[ps.shape[1], 128], [64, 4], [1, 64]])))
            for j, dc in enumerate((2 * g, 2 * g + 1)):
                def epilogue(qg, dc=dc):
                    for p in (0, 1):
                        obuf = bpC if p == 0 else bpC2
                        ooff = 0 if p == 0 else 512
                        dof = p * 2048
                        sop([obuf], [b_den[p]], lambda ooff=ooff, dof=dof: SE.copy(
                            rden.v(dof, 512, 0, 65), pC[0:65, ooff:ooff + 512]))
                        sop([b_den[p]], [b_den[p]], lambda dof=dof: SE.activation(
                            rden.v(dof, 512, 64, 65), rden.v(dof, 512, 64, 65), AF.Ln))
                        sop([b_den[p]], [b_den[p]], lambda dof=dof: SE.activation(
                            rden.v(dof, 512, 64, 65), rden.v(dof, 512, 64, 65),
                            AF.Exp, scale=-1.0))
                        top([b_den[p], b_ones], [bpD], lambda dof=dof: TE.matmul(
                            pD[0:64, 0:512], ones_f[64:65, 0:64],
                            rden.v(dof, 512, 64, 65), start=True, stop=True))
                        if p == 0:
                            vop([b_den[p], bpD], [b_ao], lambda dc=dc, qg=qg, dof=dof:
                                VE.tensor_mul(
                                    aov(dc * 4096 + qg * 1024, 512, 0, 64),
                                    rden.v(dof, 512, 0, 64), pD[0:64, 0:512]))
                        else:
                            vop([b_den[p], bpD], [b_aos], lambda qg=qg, dof=dof:
                                VE.tensor_mul(
                                    raos.v((qg % 2) * 1024, 512, 0, 64),
                                    rden.v(dof, 512, 0, 64), pD[0:64, 0:512]))
                    if qg % 2 == 1:
                        gdma([b_aos], [b_ao], lambda dc=dc, qg=qg: GE.dma_start(
                            aov(dc * 4096 + (qg - 1) * 1024, 1024, 64, 128),
                            raos.v(0, 1024, 0, 64)))

                def emit_av(prev):
                    qg, c0, n0, n1, sl, last = prev
                    nch = ext_chunks(4 * qg)
                    for p in (0, 1):
                        rp = rpe if p == 0 else rpo
                        bP = b_pe[sl] if p == 0 else b_po[sl]
                        obuf = bpC if p == 0 else bpC2
                        ooff = 0 if p == 0 else 512
                        def av1(g=g, j=j, p=p, c0=c0, n0=n0, n1=n1, sl=sl,
                                rp=rp, ooff=ooff, nch=nch):
                            hh = j * 2 + p
                            lastins = None
                            for ci, (off, w) in enumerate(((0, n0), (n0, n1))):
                                if w <= 0:
                                    continue
                                c = c0 + ci
                                lastins = TE.matmul(
                                    pC[0:65, ooff:ooff + w],
                                    rvg.v(c * 520 + hh * 130, 65),
                                    rp.v(sl * 2048 + off * 2, w),
                                    start=(c == 0), stop=(c == nch - 1))
                            return lastins
                        top([bP, b_vg], [obuf], av1)
                    if last:
                        epilogue(qg)

                prev = None
                sl = 0
                for qg in range(4):
                    nch = ext_chunks(4 * qg)
                    for c0 in range(0, nch, 2):
                        na = [128 * sum(1 for i in range(4 * qg, 4 * qg + 4)
                                        if ext_chunks(i) > c) for c in (c0, c0 + 1)]
                        n0, n1 = na
                        segs = [(0, n0, c0)]
                        if n0 >= 512 or n0 + n1 <= 512:
                            segs.append((n0, n1, c0 + 1))
                        else:
                            segs.append((n0, 512 - n0, c0 + 1))
                            segs.append((512, n0 + n1 - 512, c0 + 1))
                        for p in (0, 1):
                            S = pA if p == 0 else pB
                            bS = [bpA, bpA2] if p == 0 else [bpB, bpB2]
                            def sc_emit(j=j, dc=dc, qg=qg, p=p, S=S,
                                        segs=tuple(segs), n0=n0, c0=c0):
                                lastins = None
                                for off, w, c in segs:
                                    if w <= 0:
                                        continue
                                    qs = off if c == c0 else off - n0
                                    lastins = TE.matmul(
                                        S[:, off:off + w],
                                        kgv(j, c * 128, 128, p * 64, (p + 1) * 64),
                                        qv(dc, qg * 512 + qs, w, p * 64, (p + 1) * 64),
                                        start=True, stop=True)
                                return lastins
                            top([b_kg[j], b_q[dc]], bS, sc_emit)
                        for p in (0, 1):
                            S = pA if p == 0 else pB
                            bS = [bpA, bpA2] if p == 0 else [bpB, bpB2]
                            rp = rpe if p == 0 else rpo
                            bP = b_pe[sl] if p == 0 else b_po[sl]
                            sop(bS, [bP], lambda S=S, n=n0 + n1, rp=rp, sl=sl:
                                SE.activation(rp.v(sl * 2048, n), S[:, 0:n], AF.Exp))
                        if prev is not None:
                            emit_av(prev)
                        prev = (qg, c0, n0, n1, sl, c0 + 2 >= nch)
                        sl = 1 - sl
                emit_av(prev)
            ckpt(5 if g == 0 else 6)
            bar()

        rvg.free(); raos.free(); rpe.free(); rpo.free(); rden.free()
        rlat.free(); rq.free(); rrt.free()
        rwk.free(); rwks.free(); rwv.free(); rck.free(); rsk.free()

        rwo = Reg(K, "wo", 16384, BF16)
        b_wo = Buf()
        wload(rwo, d["i_wo"], 8, 1024, b_wo)
        rx1 = Reg(K, "x1", 16 * 4096, F32)
        rxo = Reg(K, "xor", 2 * 4096, F32)
        b_x1 = [Buf() for _ in range(16)]
        b_xor = [Buf(), Buf()]

        def x1v(tt, c0, w):
            return rx1.v(tt * 4096 + c0 * 4, w)

        def xo_dma(tt):
            sl = tt % 2
            hdma([], [b_xor[sl]], lambda tt=tt, sl=sl: SY.dma_start(
                rxo.v(sl * 4096, 1024), d["i_xown"][tt * 128:(tt + 1) * 128, :]))

        xo_dma(0)
        xo_dma(1)
        for tt in range(16):
            sl = tt % 2
            for dh in range(2):
                bA = bpA if dh == 0 else bpA2
                def op_emit(tt=tt, dh=dh):
                    last = None
                    for dc in range(8):
                        last = TE.matmul(
                            pA[:, dh * 512:(dh + 1) * 512],
                            aov(dc * 4096 + tt * 256, 128),
                            rwo.v(dc * 2048 + dh * 1024, 512),
                            start=(dc == 0), stop=(dc == 7))
                    return last
                top([b_ao, b_wo], [bA], op_emit)
                vop([bA, b_xor[sl]], [b_x1[tt]], lambda tt=tt, sl=sl, dh=dh:
                    VE.tensor_add(
                        x1v(tt, dh * 512, 512), pA[:, dh * 512:(dh + 1) * 512],
                        rxo.v(sl * 4096 + dh * 2048, 512)))
            if tt + 2 < 16:
                xo_dma(tt + 2)
            ln_tile(K, x1v, tt, sq_scr, lnt, eps_t, b_lnt, b_eps, b_x1[tt])
        ckpt(7)
        bar()
        K.A.free(rx0b.off, 6 * 8192)
        rxo.free()

        rx1b = Reg(K, "x1b", 8 * 2048, BF16)
        rxlo = Reg(K, "xlo", 8 * 1024, BF16)
        ru = Reg(K, "u", 6 * 2048, BF16)
        rw1 = [Reg(K, "w1a", 24576, BF16), Reg(K, "w1b", 24576, BF16)]
        rw2 = Reg(K, "w2", 12288, BF16)
        rgwh = Reg(K, "gwh", 128, BF16)
        rgwl = Reg(K, "gwl", 128, BF16)
        rxbtm = Reg(K, "xbtm", 2 * 2048, BF16)
        rgwtf = Reg(K, "gwtf", 256, F32)
        rsg = Reg(K, "sg", 2 * 2048, F32)
        rtmp = Reg(K, "tmp", 2 * 2048, F32)
        rwcol = Reg(K, "wcol", 512, F32)
        rgst = Reg(K, "gst", 2048, F32)
        rgsc = Reg(K, "gsc", 64, F32)
        rout = Reg(K, "out", 2 * 4096, F32)
        b_x1b, b_xlo, b_u = Buf(), Buf(), Buf()
        b_w1 = [Buf(), Buf()]
        b_w2 = Buf()
        b_gwh, b_gwl, b_gwtf, b_gst, b_gsc = Buf(), Buf(), Buf(), Buf(), Buf()
        b_xbtm = [Buf(), Buf()]
        b_sg = [Buf(), Buf()]
        b_tmp = [Buf(), Buf()]
        b_w = [Buf() for _ in range(16)]
        b_out = [Buf(), Buf()]

        hdma([], [b_gwtf], lambda: SY.dma_start(
            rgwtf.ap(0, [[8, 8], [1, 8]]),
            AP(d["i_gwt"].tensor, 0, [[8, 128], [8 * 128, 8], [1, 8]])))
        vop([b_gwtf], [b_gwh], lambda: VE.tensor_copy(
            rgwh.v(0, 64), rgwtf.v(0, 64)))
        vop([b_gwtf, b_gwh], [b_gwl], lambda: VE.tensor_sub(
            rgwl.v(0, 64), rgwtf.v(0, 64), rgwh.v(0, 64)))

        def w1_dma(slot, ei):
            if ei < 8:
                iap = AP(d["i_w1t"].tensor, ei * 1024 * 1536,
                         [[1536, 128], [1536 * 128, 8], [1, 1536]])
            else:
                iap = AP(d["i_ws1t"].tensor, (ei - 8) * 1536,
                         [[3072, 128], [3072 * 128, 8], [1, 1536]])
            oap = rw1[slot].ap(0, [[1536, 8], [1, 1536]])
            gdma([], [b_w1[slot]], lambda oap=oap, iap=iap: GE.dma_start(oap, iap))

        def w2_dma(ei):
            if ei < 8:
                iap = AP(d["i_w2t"].tensor, ei * 768 * 1024,
                         [[1024, 128], [1024 * 128, 6], [1, 1024]])
            else:
                iap = AP(d["i_ws2t"].tensor, (ei - 8) * 768 * 1024,
                         [[1024, 128], [1024 * 128, 6], [1, 1024]])
            oap = rw2.ap(0, [[1024, 6], [1, 1024]])
            gdma([], [b_w2], lambda oap=oap, iap=iap: GE.dma_start(oap, iap))

        w1_dma(0, 0)
        w2_dma(0)

        for hf in range(2):
            for tl in range(8):
                tt = hf * 8 + tl
                sl = tl % 2
                vop([b_x1[tt]], [b_xbtm[sl]], lambda tt=tt, sl=sl: VE.tensor_copy(
                    rxbtm.v(sl * 2048, 1024), x1v(tt, 0, 1024)))
                def tx_emit(sl=sl):
                    last = None
                    for c in range(8):
                        last = TE.transpose(
                            pT[:, c * 128:(c + 1) * 128],
                            rxbtm.v(sl * 2048 + c * 256, 128), idb_b[:, :])
                    return last
                top([b_xbtm[sl], b_idb], bpT, tx_emit)
                vop(bpT, [b_x1b], lambda tl=tl: VE.tensor_copy(
                    rx1b.ap(tl * 256, [[1024, 8], [1, 128]]),
                    AP(pT, 0, [[1024, 128], [128, 8], [1, 128]])))
                vop([b_x1[tt], b_xbtm[sl]], [b_xbtm[sl]], lambda tt=tt, sl=sl:
                    VE.tensor_sub(
                        rxbtm.v(sl * 2048, 1024), x1v(tt, 0, 1024),
                        rxbtm.v(sl * 2048, 1024)))
                def tlx_emit(sl=sl):
                    last = None
                    for c in range(8):
                        last = TE.transpose(
                            pT[:, c * 128:(c + 1) * 128],
                            rxbtm.v(sl * 2048 + c * 256, 128), idb_b[:, :])
                    return last
                top([b_xbtm[sl], b_idb], bpT, tlx_emit)
                vop(bpT, [b_xlo], lambda tl=tl: VE.tensor_copy(
                    rxlo.ap((tl % 4) * 256, [[512, 8], [1, 128]]),
                    AP(pT, 0, [[1024, 128], [128, 8], [1, 128]])))
                if tl % 4 == 3:
                    tc = tl // 4
                    def g_emit(tc=tc):
                        last = None
                        k = 0
                        for reg, isxlo in ((rgwh, False), (rgwl, False), (rgwh, True)):
                            for dn in range(8):
                                xap = (rxlo.v(dn * 1024, 512) if isxlo
                                       else rx1b.v(dn * 2048 + tc * 1024, 512))
                                last = TE.matmul(
                                    pD[0:8, 0:512], reg.v(dn * 16, 8), xap,
                                    start=(k == 0), stop=(k == 23))
                                k += 1
                        return last
                    top([b_gwh, b_gwl, b_x1b, b_xlo], [bpD], g_emit)
                    vop([bpD], [b_gst], lambda: VE.tensor_copy(
                        rgst.v(0, 512, 0, 8), pD[0:8, 0:512]))
                    def gt_emit():
                        last = None
                        for jj in range(4):
                            last = TE.transpose(
                                pC[:, jj * 8:(jj + 1) * 8],
                                rgst.v(jj * 512, 128, 0, 8), idb_f[0:8, 0:8])
                        return last
                    top([b_gst, b_idb], [bpC], gt_emit)
                    for jj in range(4):
                        ti = hf * 8 + tc * 4 + jj
                        gate_softmax_top2(K, pC, jj, rwcol, rgsc, ti, lnt,
                                          bpC, b_w[ti], b_gsc, b_lng)
            ckpt(8 if hf == 0 else 10)
            for ei in range(10):
                slot = ei % 2
                nxt = ei + 1 if ei < 9 else (0 if hf == 0 else None)
                if nxt is not None:
                    w1_dma(1 - slot, nxt)
                for m in range(6):
                    for tc in range(2):
                        pj = (m * 2 + tc) % 2
                        ps = pA if pj == 0 else pB
                        bY = bpA if pj == 0 else bpB
                        bG = bpA2 if pj == 0 else bpB2
                        def w1_emit(m=m, tc=tc, slot=slot, ps=ps, blk=0):
                            last = None
                            for dn in range(8):
                                last = TE.matmul(
                                    ps[:, blk * 512:(blk + 1) * 512],
                                    rw1[slot].v(dn * 3072 + (2 * m + blk) * 256, 128),
                                    rx1b.v(dn * 2048 + tc * 1024, 512),
                                    start=(dn == 0), stop=(dn == 7))
                            return last
                        top([b_w1[slot], b_x1b], [bY], lambda f=w1_emit: f())
                        top([b_w1[slot], b_x1b], [bG],
                            lambda m=m, tc=tc, slot=slot, ps=ps: w1_emit(m, tc, slot, ps, 1))
                        ssl = (m * 2 + tc) % 2
                        sop([bG], [b_sg[ssl]], lambda ps=ps, ssl=ssl: SE.activation(
                            rsg.v(ssl * 2048, 512), ps[:, 512:1024], AF.Silu))
                        vop([bY, b_sg[ssl]], [b_u], lambda m=m, tc=tc, ps=ps, ssl=ssl:
                            VE.tensor_mul(
                                ru.v(m * 2048 + tc * 1024, 512),
                                ps[:, 0:512], rsg.v(ssl * 2048, 512)))
                for tl in range(8):
                    tt = hf * 8 + tl
                    for dh in range(2):
                        pj = (tl * 2 + dh) % 2
                        bZ = bpC if pj == 0 else bpC2
                        zo = 0 if pj == 0 else 512
                        def w2_emit(tl=tl, dh=dh, zo=zo):
                            last = None
                            for m in range(6):
                                last = TE.matmul(
                                    pC[:, zo:zo + 512],
                                    ru.v(m * 2048 + tl * 256, 128),
                                    rw2.v(m * 2048 + dh * 1024, 512),
                                    start=(m == 0), stop=(m == 5))
                            return last
                        top([b_w2, b_u], [bZ], w2_emit)
                        if ei < 8:
                            ssl = (tl * 2 + dh) % 2
                            sop([bZ, b_w[tt]], [b_tmp[ssl]],
                                lambda zo=zo, ssl=ssl, tt=tt, ei=ei: SE.activation(
                                    rtmp.v(ssl * 2048, 512),
                                    pC[:, zo:zo + 512], AF.Identity,
                                    scale=rwcol.v(tt * 32 + ei * 4, 1)))
                            vop([b_tmp[ssl], b_x1[tt]], [b_x1[tt]],
                                lambda tt=tt, dh=dh, ssl=ssl: VE.tensor_add(
                                    x1v(tt, dh * 512, 512), x1v(tt, dh * 512, 512),
                                    rtmp.v(ssl * 2048, 512)))
                        else:
                            vop([bZ, b_x1[tt]], [b_x1[tt]],
                                lambda tt=tt, dh=dh, zo=zo: VE.tensor_add(
                                    x1v(tt, dh * 512, 512), pC[:, zo:zo + 512],
                                    x1v(tt, dh * 512, 512)))
                    if ei == 9:
                        sl = tl % 2
                        ln_tile(K, x1v, tt, sq_scr, lnt, eps_t, b_lnt, b_eps,
                                b_x1[tt], out=(rout.v(sl * 4096, 1024), b_out[sl]))
                        hdma([b_out[sl]], [], lambda tt=tt, sl=sl: SY.dma_start(
                            y[tt * 128:(tt + 1) * 128, :], rout.v(sl * 4096, 1024)))
                if nxt is not None:
                    w2_dma(nxt)
                ckpt(9 if (hf == 0 and ei == 0) else 99)

        ckpt(98)

    with nc.Block():
        try:
            _body()
        except StopBuild:
            pass
        for i in range(4):
            SY.wait_ge(K.sems[f"dh{i}"], K.cnt[f"dh{i}"])
            GE.wait_ge(K.sems[f"dg{i}"], K.cnt[f"dg{i}"])

    for cm in K.sem_ctx:
        cm.__exit__(None, None, None)
    return nc


def ln_tile(K, x1v, tt, sq_scr, lnt, eps_t, b_lnt_l, b_eps, b_x1, out=None):
    SE, VE = K.nc.scalar, K.nc.vector
    vop, sop = K.vop, K.sop
    b_lnt = b_lnt_l[tt % 2]
    o = (tt % 2) * 16

    def c(i):
        return lnt[:, o + i:o + i + 1]
    vop([b_x1], [b_lnt], lambda: VE.reduce_sum(c(0), x1v(tt, 0, 1024), axis=AX.X))
    sop([b_x1], [b_lnt], lambda: SE.activation(
        sq_scr[:, :], x1v(tt, 0, 1024), AF.Square, accum_out=c(1)))
    vop([b_lnt], [b_lnt], lambda: VE.tensor_scalar_mul(c(2), c(0), 1.0 / 1024.0))
    vop([b_lnt], [b_lnt], lambda: VE.tensor_scalar_mul(c(3), c(1), 1.0 / 1024.0))
    vop([b_lnt], [b_lnt], lambda: VE.tensor_mul(c(4), c(2), c(2)))
    vop([b_lnt], [b_lnt], lambda: VE.tensor_sub(c(5), c(3), c(4)))
    sop([b_lnt, b_eps], [b_lnt], lambda: SE.activation(
        c(6), c(5), AF.Ln, bias=eps_t[:, 0:1]))
    sop([b_lnt], [b_lnt], lambda: SE.activation(c(7), c(6), AF.Exp, scale=-0.5))
    vop([b_lnt], [b_lnt], lambda: K.nc.vector.scalar_tensor_tensor(
        c(8), c(2), -1.0, c(7), ALU.mult, ALU.mult))
    if out is None:
        sop([b_lnt], [b_x1], lambda: SE.activation(
            x1v(tt, 0, 1024), x1v(tt, 0, 1024), AF.Identity,
            bias=c(8), scale=c(7)))
    else:
        oap, b_o = out
        sop([b_lnt, b_x1], [b_o], lambda: SE.activation(
            oap, x1v(tt, 0, 1024), AF.Identity,
            bias=c(8), scale=c(7)))


def gate_softmax_top2(K, pC, jj, rwcol, rgsc, ti, lnt, bpC, b_w, b_gsc, b_lnt):
    VE, SE = K.nc.vector, K.nc.scalar
    vop, sop = K.vop, K.sop
    wv = lambda: rwcol.v(ti * 32, 8)
    gs = lambda: rgsc.v(0, 8)

    def c(i):
        return lnt[:, 32 + i:32 + i + 1]
    vop([bpC], [b_w], lambda: VE.tensor_copy(wv(), pC[:, jj * 8:(jj + 1) * 8]))
    vop([b_w], [b_lnt], lambda: VE.reduce_max(c(0), wv(), axis=AX.X))
    vop([b_w, b_lnt], [b_w], lambda: VE.tensor_scalar(
        wv(), wv(), c(0), None, ALU.subtract))
    sop([b_w], [b_w], lambda: SE.activation(wv(), wv(), AF.Exp))
    vop([b_w], [b_lnt], lambda: VE.reduce_sum(c(1), wv(), axis=AX.X))
    vop([b_lnt], [b_lnt], lambda: VE.reciprocal(c(2), c(1)))
    vop([b_w, b_lnt], [b_w], lambda: VE.tensor_scalar(
        wv(), wv(), c(2), None, ALU.mult))
    vop([b_w], [b_lnt], lambda: VE.reduce_max(c(3), wv(), axis=AX.X))
    vop([b_w, b_lnt], [b_gsc], lambda: VE.tensor_scalar(
        gs(), wv(), c(3), None, ALU.is_equal))
    vop([b_gsc, b_w], [b_gsc], lambda: VE.scalar_tensor_tensor(
        gs(), gs(), -1e30, wv(), ALU.mult, ALU.add))
    vop([b_gsc], [b_lnt], lambda: VE.reduce_max(c(4), gs(), axis=AX.X))
    vop([b_w, b_lnt], [b_gsc], lambda: VE.tensor_scalar(
        gs(), wv(), c(4), None, ALU.is_ge))
    vop([b_gsc, b_w], [b_w], lambda: VE.tensor_mul(wv(), wv(), gs()))


_CACHED_NC = None


def _get_nc():
    global _CACHED_NC
    if _CACHED_NC is None:
        _CACHED_NC = build_program()
    return _CACHED_NC


def _perm_deint():
    p = np.zeros(E, np.int64)
    for h in range(16):
        for jx in range(64):
            p[h * 64 + jx] = h * 64 + (2 * jx if jx < 32 else 2 * (jx - 32) + 1)
    return p


def _perm_swap():
    p = np.zeros(E, np.int64)
    for h in range(16):
        for jx in range(64):
            p[h * 64 + jx] = h * 64 + (jx + 32 if jx < 32 else jx - 32)
    return p


def kernel(L0_attn_x, L0_attn_wq, L0_attn_w_down, L0_attn_wk_up, L0_attn_wv_up,
           L0_attn_wo, L0_ffn_gate_w, L0_ffn_W1, L0_ffn_W2, L0_ffn_Ws1,
           L0_ffn_Ws2):
    x = np.asarray(L0_attn_x, np.float32)
    wq = np.asarray(L0_attn_wq, np.float32)
    wd = np.asarray(L0_attn_w_down, np.float32)
    wk = np.asarray(L0_attn_wk_up, np.float32)
    wv = np.asarray(L0_attn_wv_up, np.float32)
    wo = np.asarray(L0_attn_wo, np.float32)
    gw = np.asarray(L0_ffn_gate_w, np.float32)
    W1 = np.asarray(L0_ffn_W1, np.float32)
    W2 = np.asarray(L0_ffn_W2, np.float32)
    Ws1 = np.asarray(L0_ffn_Ws1, np.float32)
    Ws2 = np.asarray(L0_ffn_Ws2, np.float32)

    pd_ = _perm_deint()
    psw = _perm_swap()
    C = np.ascontiguousarray
    wq_p = C(wq[:, pd_])
    wq_s = C(wq_p[:, psw])
    wk_p = C(wk[:, pd_])
    wk_s = C(wk_p[:, psw])

    ro1 = np.concatenate([np.r_[j * 128:(j + 1) * 128, 768 + j * 128:768 + (j + 1) * 128]
                          for j in range(6)])
    W1t = C(np.concatenate([W1[e][ro1].T for e in range(8)], axis=0))
    W2t = C(np.concatenate([W2[e].T for e in range(8)], axis=0))
    ro2 = np.concatenate([np.r_[j * 128:(j + 1) * 128, 1536 + j * 128:1536 + (j + 1) * 128]
                          for j in range(12)])
    Ws1t = C(Ws1[ro2].T)
    Ws2t = C(Ws2.T)
    gwt = C(gw.T)
    idn = np.eye(128, dtype=np.float32)

    inv = 1.0 / (10000.0 ** (np.arange(0, 64, 2, dtype=np.float32) / 64))
    ang = np.outer(np.arange(T, dtype=np.float32), inv)
    cosf = np.cos(ang).astype(np.float32)
    sinf = np.sin(ang).astype(np.float32)
    rsel = np.arange(128) % 32
    sgn = np.where((np.arange(128) % 64) < 32, -1.0, 1.0).astype(np.float32)[:, None]

    shared = dict(i_wq=wq_p, i_wqs=wq_s, i_wd=wd, i_wk=wk_p, i_wks=wk_s,
                  i_wv=wv, i_wo=wo, i_gwt=gwt, i_w1t=W1t, i_w2t=W2t,
                  i_ws1t=Ws1t, i_ws2t=Ws2t, i_idn=idn)

    in_maps = []
    meta = []
    for core in range(8):
        seq, half = core // 2, core % 2
        idx_r = np.zeros(T, np.int64)
        for p in range(16):
            idx_r[p * 256:p * 256 + 128] = p * 256 + half * 128 + np.arange(128)
            idx_r[p * 256 + 128:(p + 1) * 256] = p * 256 + (1 - half) * 128 + np.arange(128)
        idx_own = np.concatenate([(15 - i) * 256 + np.arange(128) for i in range(16)])
        own_pos = idx_r[idx_own]
        xr = C(x[seq][idx_r])
        m = dict(shared)
        m.update(i_xseq=xr, i_xown=C(xr[idx_own]),
                 i_ck=C(cosf[idx_r].T[rsel]),
                 i_sk=C(sinf[idx_r].T[rsel] * sgn),
                 i_cq=C(0.125 * cosf[own_pos].T[rsel]),
                 i_sq=C(0.125 * sinf[own_pos].T[rsel] * sgn))
        in_maps.append(m)
        meta.append((seq, own_pos))

    nc = _get_nc()
    res = run_bass_kernel_spmd(nc, in_maps, core_ids=list(range(8)))

    ys = [np.asarray(res.results[core]["y"]) for core in range(8)]
    odt = np.float64 if ys[0].dtype == np.float64 else np.float32
    out = np.zeros((4, T, E), odt)
    for core in range(8):
        seq, pos = meta[core]
        out[seq, pos, :] = ys[core]
    return out
